# version 23
# baseline (speedup 1.0000x reference)
import numpy as np
import ml_dtypes
import concourse.bass as bass
import concourse.mybir as mybir
from concourse.bass_utils import run_bass_kernel_spmd

F32 = mybir.dt.float32
BF16 = mybir.dt.bfloat16
AF = mybir.ActivationFunctionType
ALU = mybir.AluOpType

B, SEQ, D = 4, 4096, 1024
NH, HD = 16, 64
NT = 20
NQT = 16
NGR = 4
CORE_ROWS = NT * 128
OUT_ROWS = NQT * 128
SLOPES = [2.0 ** (-(h + 1) / 2.0) for h in range(NH)]
_DLIM = {0: 1, 1: 2, 2: 2, 3: 3}


def dlim(h):
    return _DLIM.get(h, 4)


PV_HEADS = (0, 1, 2)
PV_COMBOS = [(h, d) for h in PV_HEADS for d in range(dlim(h) + 1)]
SHIFT_G = 256.0
SHIFT_V = 64.0
NEGB = -1.0e30
INV_SQRT_D = 1.0 / 32.0


def make_tables(half):
    kloc = np.arange(128, dtype=np.float64)
    btabG = np.zeros((128, 2 * 8 * NH), np.float32)
    for gcase in (0, 1):
        for op_ in range(8):
            for h in range(NH):
                col = (gcase * 8 + op_) * NH + h
                invalid = (half == 0) and (gcase == 0) and (op_ < 4)
                btabG[:, col] = NEGB if invalid else (
                    SLOPES[h] * (kloc + 128.0 * (op_ - 4) - SHIFT_G))
    btabV = np.zeros((128, 2 * len(PV_COMBOS)), np.float32)
    for case in range(2):
        for i, (h, d) in enumerate(PV_COMBOS):
            col = case * len(PV_COMBOS) + i
            invalid = (half == 0) and (case == 0)
            btabV[:, col] = NEGB if invalid else (
                SLOPES[h] * (kloc - 128.0 * d - SHIFT_V))
    k = np.arange(128)[:, None]
    q = np.arange(128)[None, :]
    mdiag = (k <= q).astype(np.float32)
    medge = (k >= q).astype(np.float32)
    return btabG, btabV, mdiag, medge


class IR:
    def __init__(self):
        self.ops = []
        self.ctr = {}
        self.events = {}

    def add(self, eng, emit, waits=(), inc=None, ev=None):
        rw = []
        for w in waits:
            if w is None:
                continue
            if isinstance(w, str):
                assert w in self.events, f"event {w} not yet defined"
                rw.append(self.events[w])
            else:
                rw.append(w)
        ri = []
        if inc is not None:
            sem, amt = inc
            self.ctr[sem] = self.ctr.get(sem, 0) + amt
            ri.append((sem, amt))
            if ev is not None:
                self.events[ev] = (sem, self.ctr[sem])
        else:
            assert ev is None
        self.ops.append((eng, rw, ri, emit))

    def cum(self, sem):
        return self.ctr.get(sem, 0)


def build_nc():
    import os
    PHASE = int(os.environ.get("KDBG_PHASE", "3"))
    HC = int(os.environ.get("KDBG_HC", "16"))
    nc = bass.Bass("TRN2", target_bir_lowering=False)

    dt = nc.dram_tensor
    xin = dt("xin", [CORE_ROWS, D], F32, kind="ExternalInput").ap()
    wkqv = dt("wkqv", [D, 3 * D], F32, kind="ExternalInput").ap()
    wmod = dt("wmod", [D, 6 * D], F32, kind="ExternalInput").ap()
    fker = dt("fker", [D, D], F32, kind="ExternalInput").ap()
    fbias = dt("fbias", [1, D], F32, kind="ExternalInput").ap()
    sin_d = dt("sin", [1, 1], F32, kind="ExternalInput").ap()
    sk_d = dt("sk", [1, 1], F32, kind="ExternalInput").ap()
    cfm_d = dt("cfm", [128, 8], F32, kind="ExternalInput").ap()
    btg_d = dt("btg", [128, 256], F32, kind="ExternalInput").ap()
    btv_d = dt("btv", [128, 2 * len(PV_COMBOS)], F32, kind="ExternalInput").ap()
    md_d = dt("mdiag", [128, 128], F32, kind="ExternalInput").ap()
    me_d = dt("medge", [128, 128], F32, kind="ExternalInput").ap()
    ng_d = dt("negi", [128, 128], F32, kind="ExternalInput").ap()
    id_d = dt("ident1", [1, 1], F32, kind="ExternalInput").ap()
    yout = dt("yout", [OUT_ROWS, D], F32, kind="ExternalOutput").ap()

    import contextlib
    st = contextlib.ExitStack()
    sbt = lambda name, shape, dtp: st.enter_context(nc.sbuf_tensor(name, shape, dtp))
    pst = lambda name, shape: st.enter_context(nc.psum_tensor(name, shape, F32))

    with st:
        XR = sbt("XR", [128, 8 * 1024], F32)
        WQ = sbt("WQ", [128, 8 * 3072], BF16)
        WF = sbt("WF", [128, 8 * 1024], BF16)
        KR = sbt("KR", [128, 8 * 1024], BF16)
        VR = sbt("VR", [128, 8 * 1056], BF16)
        QG = sbt("QG", [128, 4 * 1024], BF16)
        HFM = sbt("HFM", [128, 4 * 1024], BF16)
        XN = sbt("XN", [128, 1024], BF16)
        X2N = sbt("X2N", [128, 1024], BF16)
        H2F = sbt("H2F", [128, 2 * 1024], BF16)
        PBUF = sbt("PBUF", [128, 4 * 512], BF16)
        ET = sbt("ET", [128, 128], BF16)
        ZB = sbt("ZB", [128, 128], BF16)
        IDB = sbt("IDB", [1, 1], BF16)
        MODF = sbt("MODF", [1, 128], F32)
        UB = sbt("UB", [128, 16 * 512], BF16)
        UT = sbt("UT", [128, 4 * 1024], BF16)
        LROW = sbt("LROW", [65, 2 * 512], F32)
        LT = sbt("LT", [16, 512], F32)
        LTR = sbt("LTR", [16, 512], BF16)
        MODROW = sbt("MODROW", [1, 1024], BF16)
        SCRA = sbt("SCRA", [1, 1024], F32)
        SCRB = sbt("SCRB", [1, 1024], F32)
        EGA = sbt("EGA", [16, 1024], BF16)
        GBR = sbt("GBR", [128, 1024], BF16)
        BIASR = sbt("BIASR", [128, 1024], BF16)
        CAF = sbt("CAF", [128, 8], F32)
        SAF = sbt("SAF", [128, 8], F32)
        CBF = sbt("CBF", [128, 8], F32)
        SBF = sbt("SBF", [128, 8], F32)
        SEF = sbt("SEF", [128, 8], F32)
        SCF = sbt("SCF", [128, 8], F32)
        SCB = sbt("SCB", [128, 8], BF16)
        BTG = sbt("BTG", [128, 256], F32)
        BTV = sbt("BTV", [128, 2 * len(PV_COMBOS)], F32)
        MD = sbt("MD", [128, 128], BF16)
        ME = sbt("ME", [128, 128], BF16)
        NG = sbt("NG", [128, 128], BF16)
        IDENT = sbt("IDENT", [1, 1], F32)
        SMALL = sbt("SMALL", [1, 8], F32)
        SSV = sbt("SSV", [128, 1], F32)
        RSIN = sbt("RSIN", [128, 1], F32)
        RSK = sbt("RSK", [128, 1], F32)
        BST = sbt("BST", [128, 12], F32)
        MV = sbt("MV", [128, 2], F32)
        LNT = sbt("LNT", [128, 1], F32)
        RV = sbt("RV", [128, 1], F32)
        NMR = sbt("NMR", [128, 1], F32)
        BST2 = sbt("BST2", [128, 12], F32)
        MV2 = sbt("MV2", [128, 2], F32)
        LNT2 = sbt("LNT2", [128, 1], F32)
        RV2 = sbt("RV2", [128, 1], F32)
        NMR2 = sbt("NMR2", [128, 1], F32)
        LNSCR = ((BST, MV, LNT, RV, NMR), (BST2, MV2, LNT2, RV2, NMR2))
        EPSV = sbt("EPSV", [128, 1], F32)
        ONES1 = sbt("ONES1", [1, 128], F32)

        PS = {"PA": pst("PA", [128, 1024]),
              "PB2": pst("PB2", [128, 1024]),
              "PC": pst("PC", [128, 512]),
              "PD": pst("PD", [128, 512]),
              "PE": pst("PE", [128, 512]),
              "PF2": pst("PF2", [128, 512])}
        PA, PB2, PC, PD, PE, PF2 = (PS[k] for k in
                                    ("PA", "PB2", "PC", "PD", "PE", "PF2"))

        sem_names = (["sx%d" % i for i in range(8)] +
                     ["sw", "swm0", "swm1", "sst", "sT", "sV", "sS",
                      "syxn0", "syxn1", "syxna0", "syxna1",
                      "syu", "syl0", "syl1",
                      "syx20", "syx21", "syx2a0", "syx2a1", "symisc", "sg"])
        sems = {n: st.enter_context(nc.semaphore(n)) for n in sem_names}

        ir = IR()
        ps_readers = {k: [] for k in PS}
        ps_readers["PA0"] = []
        ps_readers["PA1"] = []
        ps_readers["PB2a"] = []
        ps_readers["PB2b"] = []
        del ps_readers["PA"]
        del ps_readers["PB2"]

        def pwrite(name):
            ws = ps_readers[name]
            ps_readers[name] = []
            return list(ws)

        def pread(name, evname):
            if evname is not None:
                ps_readers[name].append(evname)

        pb_readers = {i: [] for i in range(4)}
        et_last = [None]

        WQr = WQ.rearrange("p (k n) -> p k n", k=8)
        WFr = WF.rearrange("p (k n) -> p k n", k=8)
        KRr = KR.rearrange("p (s j r) -> p s j r", s=8, j=8)
        VRr = VR.rearrange("p (s h c) -> p s h c", s=8, h=16)
        QGr = QG.rearrange("p (t j r) -> p t j r", t=4, j=8)
        HFMr = HFM.rearrange("p (t j r) -> p t j r", t=4, j=8)
        XRr = XR.rearrange("p (s n) -> p s n", s=8)
        H2Fr = H2F.rearrange("p (b j r) -> p b j r", b=2, j=8)
        PBr = PBUF.rearrange("p (b n) -> p b n", b=4)
        UBr = UB.rearrange("p (h n) -> p h n", h=16)
        UTr = UT.rearrange("p (t c) -> p t c", t=4)
        LRr = LROW.rearrange("p (b n) -> p b n", b=2)

        xld_count = [0] * 8

        def load_x(t, waits=()):
            s = t % 8
            xld_count[s] += 1
            ir.add("G", lambda g, t=t, s=s: g.dma_start(
                out=XRr[:, s, :], in_=xin[128 * t:128 * (t + 1), :]),
                waits=waits, inc=("sx%d" % s, 16))

        for t in range(8):
            load_x(t)
        for k8 in range(8):
            ir.add("G", lambda g, k8=k8: g.dma_start(
                out=WQr[:, k8, :], in_=wkqv[128 * k8:128 * (k8 + 1), :]),
                inc=("sw", 16))
        for k8 in range(8):
            ir.add("G", lambda g, k8=k8: g.dma_start(
                out=WFr[:, k8, :], in_=fker[128 * k8:128 * (k8 + 1), :]),
                inc=("sw", 16))
        for src, dst in ((cfm_d, SCF), (btg_d, BTG), (btv_d, BTV),
                         (md_d, MD), (me_d, ME), (ng_d, NG), (id_d, IDENT)):
            ir.add("G", lambda g, src=src, dst=dst: g.dma_start(
                out=dst[:, :], in_=src), inc=("sw", 16))
        ir.add("G", lambda g: g.dma_start(out=SMALL[0:1, 0:1], in_=sin_d),
               inc=("sw", 16))
        ir.add("G", lambda g: g.dma_start(out=SMALL[0:1, 1:2], in_=sk_d),
               inc=("sw", 16))
        ir.add("G", lambda g: g.dma_start(out=SCRB[0:1, :], in_=fbias),
               inc=("sw", 16), ev="fb_row")
        w_all = ("sw", ir.cum("sw"))

        ir.add("V", lambda v: v.memset(VRr[:, :, :, 64:65], 1.0), inc=("sV", 1))
        ir.add("V", lambda v: v.memset(EPSV[:, :], 1e-5), inc=("sV", 1), ev="eps_set")
        ir.add("V", lambda v: v.memset(ONES1[0:1, :], 1.0), inc=("sV", 1), ev="ones_set")
        ir.add("V", lambda v: v.memset(ZB[:, :], 0.0), inc=("sV", 1), ev="zb_set")
        ir.add("V", lambda v: v.memset(IDB[0:1, :], 1.0), inc=("sV", 1), ev="idb_set")
        ir.add("S", lambda s: s.nop(), inc=("sS", 1))


        def replicate_row(src_row, dst, n, waits, evname):
            nmm = (n + 511) // 512
            for i in range(nmm):
                w = list(waits) + ["ones_set"] + (pwrite("PF2") if i == 0
                                                  else ["%s_e%d" % (evname, i - 1)])
                c0, c1 = 512 * i, min(n, 512 * (i + 1))
                ir.add("T", lambda te, c0=c0, c1=c1: te.matmul(
                    PF2[:, 0:c1 - c0] if False else PF2[:, c0 % 512:(c0 % 512) + (c1 - c0)],
                    ONES1[0:1, :], src_row[0:1, c0:c1],
                    start=True, stop=True),
                    waits=w, inc=("sT", 1),
                    ev="%s_mm%d" % (evname, i))
                ir.add("V", lambda v, c0=c0, c1=c1: v.tensor_copy(
                    dst[:, c0:c1], PF2[:, c0 % 512:(c0 % 512) + (c1 - c0)]),
                    waits=["%s_mm%d" % (evname, i)], inc=("sV", 1),
                    ev=evname if i == nmm - 1 else ("%s_e%d" % (evname, i)))
                pread("PF2", evname if i == nmm - 1 else ("%s_e%d" % (evname, i)))

        ir.add("S", lambda s: s.activation(SEF[:, :], SCF[:, :], AF.Exp,
                                           bias=0.0, scale=-1.0),
               waits=[w_all], inc=("sS", 1), ev="silu_e")
        ir.add("V", lambda v: v.tensor_scalar(SEF[:, :], SEF[:, :], 1.0, None,
                                              op0=ALU.add),
               waits=["silu_e"], inc=("sV", 1), ev="sil1")
        ir.add("V", lambda v: v.reciprocal(SEF[:, :], SEF[:, :]),
               waits=["sil1"], inc=("sV", 1), ev="sil2")
        ir.add("V", lambda v: v.tensor_tensor(SCF[:, :], SCF[:, :], SEF[:, :],
                                              op=ALU.mult),
               waits=["sil2"], inc=("sV", 1), ev="sil3")
        ir.add("V", lambda v: v.tensor_copy(SCB[:, :], SCF[:, :]),
               waits=["sil3"], inc=("sV", 1), ev="silu_done")
        ir.add("V", lambda v: v.reciprocal(SMALL[0:1, 2:3], SMALL[0:1, 0:1]),
               waits=[w_all], inc=("sV", 1))
        ir.add("V", lambda v: v.reciprocal(SMALL[0:1, 3:4], SMALL[0:1, 1:2]),
               inc=("sV", 1))
        ir.add("V", lambda v: v.tensor_tensor(SMALL[0:1, 4:5], SMALL[0:1, 0:1],
                                              SMALL[0:1, 1:2], op=ALU.mult),
               inc=("sV", 1), ev="small_done")
        replicate_row(SMALL[0:1, 2:3], RSIN, 1, ["small_done"], "rsin_rep")
        replicate_row(SMALL[0:1, 3:4], RSK, 1, ["small_done"], "rsk_rep")
        replicate_row(SMALL[0:1, 4:5], SSV, 1, ["small_done"], "bcast_done")
        replicate_row(SCRB, BIASR, 1024, ["fb_row"], "biasr_done")
        def emit_modsection():
            ir.add("V", lambda v: v.tensor_scalar(WF[:, :], WF[:, :], RSK[:, :],
                                                  None, op0=ALU.mult),
                   waits=["rsk_rep", w_all], inc=("sV", 1), ev="wf_scaled")

            vec_psums = [PA[0:1, 0:512], PA[0:1, 512:1024],
                         PB2[0:1, 0:512], PB2[0:1, 512:1024],
                         PC[0:1, 0:512], PD[0:1, 0:512]]

            def wmod_stage(ci):
                par = ci % 2
                return UB[:, 3072 * par:3072 * (par + 1)]

            for half in range(2):
                for k8 in range(8):
                    ci = half * 8 + k8
                    par = ci % 2
                    dw = [] if ci < 2 else ["mmv_%d" % (ci - 2)]
                    ir.add("G", lambda g, half=half, k8=k8, ci=ci: g.dma_start(
                        out=wmod_stage(ci),
                        in_=wmod[128 * k8:128 * (k8 + 1),
                                 3072 * half:3072 * (half + 1)]),
                        waits=dw, inc=("swm%d" % par, 16))
                    for j in range(6):
                        w = [("swm%d" % par, 16 * (ci // 2 + 1))]
                        if ci == 0 and j == 0:
                            w += ["silu_done", w_all]
                        last = (j == 5)
                        ir.add("T", lambda te, k8=k8, j=j, ci=ci: te.matmul(
                            vec_psums[j], SCB[:, k8:k8 + 1],
                            wmod_stage(ci)[:, 512 * j:512 * (j + 1)],
                            start=(k8 == 0), stop=(k8 == 7)),
                            waits=w, inc=("sT", 1) if last else None,
                            ev=("mmv_%d" % ci) if last else None)
                for vv in range(3):
                    vec = half * 3 + vv
                    evw = ["mmv_%d" % (half * 8 + 7)]
                    ir.add("V", lambda v, vv=vv: v.tensor_copy(
                        MODROW[0:1, 0:512], vec_psums[2 * vv]),
                        waits=evw, inc=("sV", 1))
                    ir.add("V", lambda v, vv=vv: v.tensor_copy(
                        MODROW[0:1, 512:1024], vec_psums[2 * vv + 1]),
                        inc=("sV", 1), ev="modrow_%d" % vec)
                    for kk in (("PA0", "PA1"), ("PB2a", "PB2b"), ("PC", "PD"))[vv]:
                        pread(kk, "modrow_%d" % vec)
                    if vec in (0, 1, 3, 4):
                        dstv = {0: SAF, 1: CAF, 3: SBF, 4: CBF}[vec]
                        for j in range(8):
                            wv = ["modrow_%d" % vec] if j == 0 else []
                            if j > 0:
                                wv.append("vtrj_%d_%d" % (vec, j - 1))
                            ir.add("V", lambda v, j=j: v.tensor_copy(
                                MODF[0:1, :], MODROW[0:1, 128 * j:128 * (j + 1)]),
                                waits=wv, inc=("sV", 1), ev="modf_%d_%d" % (vec, j))
                            w = ["modf_%d_%d" % (vec, j)]
                            if j == 0:
                                w += pwrite("PE")
                            ir.add("T", lambda te, j=j: te.transpose(
                                PE[:, j:j + 1], MODF[0:1, :],
                                IDENT[0:1, 0:1]),
                                waits=w, inc=("sT", 1),
                                ev="vtrj_%d_%d" % (vec, j) if j < 7 else ("vtr_%d" % vec))
                        ir.add("V", lambda v, dstv=dstv: v.tensor_copy(
                            dstv[:, :], PE[:, 0:8]),
                            waits=["vtr_%d" % vec], inc=("sV", 1),
                            ev="vfm_%d" % vec)
                        pread("PE", "vfm_%d" % vec)
                    else:
                        cw = ["modrow_%d" % vec, "biasr_done_mm1"]
                        if vec == 5:
                            cw.append("ega_done")
                        ir.add("V", lambda v: v.memset(SCRB[0:1, :], 0.0),
                               waits=cw, inc=("sV", 1), ev="cl_%d_-1" % vec)
                        cseq = [
                            lambda v: v.tensor_tensor(SCRA[0:1, :], MODROW[0:1, :],
                                                      SCRB[0:1, :], op=ALU.is_ge),
                            lambda v: v.tensor_scalar(SCRA[0:1, :], SCRA[0:1, :],
                                                      2.0, -1.0, op0=ALU.mult,
                                                      op1=ALU.add),
                            lambda v: v.tensor_scalar(SCRB[0:1, :], MODROW[0:1, :],
                                                      -1.0, None, op0=ALU.mult),
                            lambda v: v.tensor_tensor(SCRB[0:1, :], MODROW[0:1, :],
                                                      SCRB[0:1, :], op=ALU.max),
                            lambda v: v.tensor_scalar(SCRB[0:1, :], SCRB[0:1, :],
                                                      -0.25, None, op0=ALU.add),
                            lambda v: v.tensor_relu(SCRB[0:1, :], SCRB[0:1, :]),
                            lambda v: v.tensor_scalar(SCRB[0:1, :], SCRB[0:1, :],
                                                      0.25, None, op0=ALU.add),
                            lambda v: v.tensor_tensor(SCRA[0:1, :], SCRA[0:1, :],
                                                      SCRB[0:1, :], op=ALU.mult),
                        ]
                        for ci2, fn in enumerate(cseq):
                            ir.add("V", fn,
                                   waits=["cl_%d_%d" % (vec, ci2 - 1)],
                                   inc=("sV", 1),
                                   ev=("clamp_%d" % vec) if ci2 == len(cseq) - 1
                                   else "cl_%d_%d" % (vec, ci2))
                        if vec == 2:
                            ir.add("G", lambda g: g.memset(EGA[:, :], 0.0),
                                   inc=("sg", 1), ev="ega_ms")
                            for h in range(16):
                                ir.add("G", lambda g, h=h: g.dma_start(
                                    out=EGA[h:h + 1, 64 * h:64 * (h + 1)],
                                    in_=SCRA[0:1, 64 * h:64 * (h + 1)]),
                                    waits=(["clamp_2", "ega_ms", "ln1_tr_3"]
                                           if h == 0 else ()),
                                    inc=("symisc", 16),
                                    ev="ega_done" if h == 15 else None)
                        else:
                            replicate_row(SCRA, GBR, 1024, ["clamp_5"],
                                          "gbr_done")
            ir.add("V", lambda v: v.tensor_scalar(CAF[:, :], CAF[:, :], 1.0, None,
                                                  op0=ALU.add),
                   waits=["vfm_1"], inc=("sV", 1))
            ir.add("V", lambda v: v.tensor_scalar(CBF[:, :], CBF[:, :], 1.0,
                                                  RSIN[:, :], op0=ALU.add,
                                                  op1=ALU.mult),
                   waits=["vfm_4", "rsin_rep"], inc=("sV", 1))
            ir.add("V", lambda v: v.tensor_scalar(SBF[:, :], SBF[:, :], RSIN[:, :],
                                                  None, op0=ALU.mult),
                   waits=["vfm_3"], inc=("sV", 1), ev="mod_ready")

        def ln_ops(t, src_ap, dst_ap, fm_dst, syp, prefix, sid=0,
                   extra_v_waits=(), extra_sy_waits=(), guard=True):
            BST_, MV_, LNT_, RV_, NMR_ = LNSCR[sid]
            ops = []
            prev_tr = "%s_tr_%d" % (prefix, t - 1)
            xnw = [prev_tr] if (guard and prev_tr in ir.events) else []
            ops.append(lambda: ir.add(
                "V", lambda v: v.bn_stats(BST_[:, 0:6], src_ap[:, 0:512]),
                waits=list(extra_v_waits), inc=("sV", 1)))
            ops.append(lambda: ir.add(
                "V", lambda v: v.bn_stats(BST_[:, 6:12], src_ap[:, 512:1024]),
                inc=("sV", 1), ev="%s_bst_%d" % (prefix, t)))
            ops.append(lambda: ir.add(
                "V", lambda v: v.bn_aggr(MV_[:, :], BST_[:, :]),
                waits=["%s_bst_%d" % (prefix, t)],
                inc=("sV", 1), ev="%s_mv_%d" % (prefix, t)))
            ops.append(lambda: ir.add(
                "S", lambda s: s.activation(LNT_[:, :], MV_[:, 1:2], AF.Ln,
                                            bias=EPSV[:, :], scale=1.0),
                waits=["%s_mv_%d" % (prefix, t), "eps_set"], inc=("sS", 1),
                ev="%s_lnt_%d" % (prefix, t)))
            ops.append(lambda: ir.add(
                "S", lambda s: s.activation(RV_[:, :], LNT_[:, :], AF.Exp,
                                            bias=0.0, scale=-0.5),
                waits=["%s_lnt_%d" % (prefix, t)],
                inc=("sS", 1), ev="%s_rv_%d" % (prefix, t)))
            ops.append(lambda: ir.add(
                "V", lambda v: v.tensor_scalar(NMR_[:, :], MV_[:, 0:1],
                                               RV_[:, :], -1.0,
                                               op0=ALU.mult, op1=ALU.mult),
                waits=["%s_rv_%d" % (prefix, t)], inc=("sV", 1),
                ev="%s_nmr_%d" % (prefix, t)))
            prev_tra = "%s_tra_%d" % (prefix, t - 1)
            xnaw = [prev_tra] if (guard and prev_tra in ir.events) else []
            ops.append(lambda: ir.add(
                "V", lambda v: v.tensor_scalar(dst_ap[:, 0:512],
                                               src_ap[:, 0:512], RV_[:, :],
                                               NMR_[:, :], op0=ALU.mult,
                                               op1=ALU.add),
                waits=xnaw + ["%s_nmr_%d" % (prefix, t)],
                inc=("sV", 1), ev="%s_xna_%d" % (prefix, t)))
            ops.append(lambda: ir.add(
                "SY", lambda sy: sy.dma_start_transpose(
                    fm_dst[:, 0:4, :], dst_ap[:, 0:512]),
                waits=["%s_xna_%d" % (prefix, t)] + list(extra_sy_waits),
                inc=(syp[:-1] + "a" + syp[-1], 16),
                ev="%s_tra_%d" % (prefix, t)))
            ops.append(lambda: ir.add(
                "V", lambda v: v.tensor_scalar(dst_ap[:, 512:1024],
                                               src_ap[:, 512:1024], RV_[:, :],
                                               NMR_[:, :], op0=ALU.mult,
                                               op1=ALU.add),
                waits=xnw, inc=("sV", 1), ev="%s_xn_%d" % (prefix, t)))
            ops.append(lambda: ir.add(
                "SY", lambda sy: sy.dma_start_transpose(
                    fm_dst[:, 4:8, :], dst_ap[:, 512:1024]),
                waits=["%s_xn_%d" % (prefix, t)],
                inc=(syp, 16), ev="%s_tr_%d" % (prefix, t)))
            return ops

        def mod_ops(t, fm_view, cvec, svec, tr_ev, prefix):
            tra_ev = tr_ev.replace("_tr_", "_tra_")
            ops = []
            for j in range(8):
                if j == 0:
                    w = [tra_ev, "mod_ready"]
                elif j == 4:
                    w = [tr_ev]
                else:
                    w = ()
                ops.append(lambda j=j, w=w: ir.add(
                    "V", lambda v, j=j: v.tensor_scalar(
                        fm_view[:, j, :], fm_view[:, j, :],
                        cvec[:, j:j + 1], svec[:, j:j + 1],
                        op0=ALU.mult, op1=ALU.add),
                    waits=w,
                    inc=("sV", 1) if j == 7 else None,
                    ev=("%s_mod_%d" % (prefix, t)) if j == 7 else None))
            return ops

        def emit_zip(la, lb):
            for i in range(max(len(la), len(lb))):
                if i < len(la):
                    la[i]()
                if i < len(lb):
                    lb[i]()
        def ln_pipeline(t, src_ap, dst_ap, fm_dst, syp, prefix,
                        extra_v_waits=(), extra_sy_waits=()):
            for f in ln_ops(t, src_ap, dst_ap, fm_dst, syp, prefix, 0,
                            extra_v_waits, extra_sy_waits):
                f()

        def modulate(t, fm_view, cvec, svec, tr_ev, prefix):
            for f in mod_ops(t, fm_view, cvec, svec, tr_ev, prefix):
                f()

        n_store = 0

        def a_tile_ops(gg, ti, extra_sy, sid=0):
            gt0a = 4 * (gg + 1)
            t = gt0a + ti
            s = t % 8
            xw = [("sx%d" % s, 16 * xld_count[s])]
            esy = list(extra_sy)
            if t == 4:
                esy.append("ega_done")
            return (ln_ops(t, XRr[:, s, :], XN[:, :],
                           HFMr[:, ti, :, :], "syxn%d" % (t % 2), "ln1", sid,
                           xw, esy) +
                    mod_ops(t, HFMr[:, ti, :, :], CAF, SAF,
                            "ln1_tr_%d" % t, "h1"))

        def emit_A_ln(gg, ti, extra_sy):
            gt0a = 4 * (gg + 1)
            t = gt0a + ti
            s = t % 8
            xw = [("sx%d" % s, 16 * xld_count[s])]
            esy = list(extra_sy)
            if t == 4:
                esy.append("ega_done")
            ln_pipeline(t, XRr[:, s, :], XN[:, :],
                        HFMr[:, ti, :, :], "syxn%d" % (t % 2), "ln1",
                        extra_v_waits=xw, extra_sy_waits=esy)

        def emit_A_tile(gg, ti, extra_sy):
            for f in a_tile_ops(gg, ti, extra_sy):
                f()

        def emit_A(gg):
            gt0a = 4 * (gg + 1)
            extra_sy = ["Bdone_%d" % (gg - 1)] if gg >= 0 else []
            for ti in range(4):
                emit_A_ln(gg, ti, extra_sy)
            for ti in range(4):
                t = gt0a + ti
                modulate(t, HFMr[:, ti, :, :], CAF, SAF, "ln1_tr_%d" % t, "h1")

        def emit_B_kq(gb, mis):
            gt0b = 4 * (gb + 1)
            tilesb = list(range(gt0b, gt0b + 4))
            hfm_ready = ["h1_mod_%d" % t for t in tilesb]
            for m in mis:
                pgn = "PA%d" % (m % 2)
                pg = PA[:, 0:512] if (m % 2 == 0) else PA[:, 512:1024]
                for k8 in range(8):
                    w = []
                    if k8 == 0:
                        w = pwrite(pgn)
                        if m == 0:
                            w += hfm_ready + [w_all]
                    ir.add("T", lambda te, m=m, k8=k8, pg=pg: te.matmul(
                        pg, WQr[:, k8, 128 * m:128 * (m + 1)],
                        HFMr[:, :, k8, :], start=(k8 == 0), stop=(k8 == 7)),
                        waits=w, inc=("sT", 1) if k8 == 7 else None,
                        ev=("qkmm_%d_%d" % (gb, m)) if k8 == 7 else None)
                if m < 8:
                    dst = KRr[:, (gt0b % 8):(gt0b % 8) + 4, m, :]
                else:
                    dst = QGr[:, :, m - 8, :]
                evw = ["qkmm_%d_%d" % (gb, m)]
                if m == 0 and gb >= 1 and ("attn_done_%d" % (gb - 1)) in ir.events:
                    evw.append("attn_done_%d" % (gb - 1))
                ir.add("S", lambda s_, dst=dst, pg=pg: s_.copy(
                    dst, pg.rearrange("p (t r) -> p t r", t=4)),
                    waits=evw, inc=("sS", 1),
                    ev="qkev_%d_%d" % (gb, m))
                pread(pgn, "qkev_%d_%d" % (gb, m))

        def emit_B_v(gb, tis):
            gt0b = 4 * (gb + 1)
            tilesb = list(range(gt0b, gt0b + 4))
            nm = 8 if gb < 0 else 16
            for ti in tis:
                t = tilesb[ti]
                for k8 in range(8):
                    for half2 in range(2):
                        pvn = "PC" if half2 == 0 else "PD"
                        pv_ = PS[pvn]
                        w = []
                        if k8 == 0:
                            w = pwrite(pvn)
                            if half2 == 0 and ti == 0:
                                w.append("qkmm_%d_%d" % (gb, nm - 1))
                        ir.add("T", lambda te, ti=ti, k8=k8, half2=half2, pv_=pv_: te.matmul(
                            pv_[:, :],
                            HFMr[:, ti, k8, :],
                            WQr[:, k8, 2048 + 512 * half2:2048 + 512 * (half2 + 1)],
                            start=(k8 == 0), stop=(k8 == 7)),
                            waits=w,
                            inc=("sT", 1) if (k8 == 7 and half2 == 1) else None,
                            ev=("vmm_%d_%d" % (gb, ti)) if (k8 == 7 and half2 == 1) else None)
                s = t % 8
                evw = ["vmm_%d_%d" % (gb, ti)]
                if ti == 0 and gb >= 1 and ("attn_done_%d" % (gb - 1)) in ir.events:
                    evw.append("attn_done_%d" % (gb - 1))
                ir.add("S", lambda s_, s=s: s_.copy(
                    VRr[:, s, 0:8, 0:64],
                    PC.rearrange("p (h c) -> p h c", h=8)),
                    waits=evw, inc=("sS", 1))
                pread("PC", None)
                ir.add("S", lambda s_, s=s: s_.copy(
                    VRr[:, s, 8:16, 0:64],
                    PD.rearrange("p (h c) -> p h c", h=8)),
                    inc=("sS", 1), ev="vev_%d_%d" % (gb, ti))
                pread("PC", "vev_%d_%d" % (gb, ti))
                pread("PD", "vev_%d_%d" % (gb, ti))

        def emit_B_done(gb):
            nm = 8 if gb < 0 else 16
            ir.add("T", lambda te: te.nop(),
                   waits=["qkev_%d_%d" % (gb, nm - 1), "vev_%d_%d" % (gb, 3)],
                   inc=("sT", 1), ev="Bdone_%d" % gb)

        def emit_B(gb):
            nm = 8 if gb < 0 else 16
            emit_B_kq(gb, list(range(nm)))
            emit_B_v(gb, [0, 1, 2, 3])
            emit_B_done(gb)

        for ti_ in range(4):
            emit_A_ln(-1, ti_, [])
        emit_modsection()
        for ti_ in range(4):
            modulate(ti_, HFMr[:, ti_, :, :], CAF, SAF,
                     "ln1_tr_%d" % ti_, "h1")
        emit_B(-1)
        for t in range(8, 12):
            load_x(t, waits=["ln1_xn_%d" % (t - 8)])
        emit_A(0)
        emit_B(0)
        emit_A(1)

        for g in range(0, NGR):
            gt0 = 4 * (g + 1)
            tiles = list(range(gt0, gt0 + 4))
            jt0 = gt0
            last_qkev = "qkev_%d_%d" % (g, 15)


            if PHASE < 2:
                for ti, t in enumerate(tiles):
                    s = t % 8
                    n_store += 1
                    ir.add("G", lambda g_, t=t, s=s: g_.dma_start(
                        out=yout[128 * (t - 4):128 * (t - 3), :], in_=XRr[:, s, :]),
                        waits=["ln1_xn_%d" % t, "vev_%d_%d" % (g, 3)],
                        inc=("sst", 16))
                for t in range(gt0 + 8, min(gt0 + 12, NT)):
                    load_x(t, waits=[("sst", 16 * n_store)])
                continue
            pvlen = len(PV_COMBOS)

            def sc_bank(h, kt):
                i = (kt % 2) * 2 + (h % 2)
                return (("PC", PS["PC"][:, :]), ("PD", PS["PD"][:, :]),
                        ("PB2a", PB2[:, 0:512]), ("PB2b", PB2[:, 512:1024]))[i]

            def pb_blk(h, kt):
                return (kt % 2) * 2 + (h % 2)

            def head_kts(h):
                if h in PV_HEADS:
                    return list(range(gt0 - dlim(h), gt0 + 4))
                if dlim(h) == 4:
                    return ([gt0, gt0 - 3, gt0 - 2, gt0 - 1, gt0 - 4] +
                            list(range(gt0 + 1, gt0 + 4)))
                return ([gt0] + list(range(gt0 - dlim(h), gt0)) +
                        list(range(gt0 + 1, gt0 + 4)))

            av_state = {}

            def emit_zero(h, first):
                pavn = "PE" if (h % 2 == 0) else "PF2"
                pav = PS[pavn]
                zw = ["zb_set"] + pwrite(pavn)
                if first:
                    zw += [last_qkev, "vev_%d_%d" % (g, 3)]
                ir.add("T", lambda te, pav=pav, h=h, gt0=gt0: te.matmul(
                    pav[0:65, 0:512], VRr[:, gt0 % 8, h, 0:65],
                    ZB[:, :].unsqueeze(1).broadcast_to([128, 4, 128]),
                    start=True, stop=False),
                    waits=zw, inc=("sT", 1))
                av_state[h] = True

            def emit_qk(h, ki, kt, first):
                qlo = max(jt0, kt)
                qhi = min(jt0 + 3, kt + dlim(h))
                nq = qhi - qlo + 1
                q0 = qlo - jt0
                pscn, psc = sc_bank(h, kt)
                w = pwrite(pscn)
                if first:
                    w += [last_qkev, "vev_%d_%d" % (g, 3)]
                fin = "qk_%d_%d_%d" % (g, h, kt)
                ir.add("T", lambda te, h=h, kt=kt, q0=q0, nq=nq, psc=psc: te.matmul(
                    psc[:, 128 * q0:128 * (q0 + nq)],
                    KRr[64 * (h % 2):64 * (h % 2) + 64, kt % 8, h // 2, :],
                    QGr[64 * (h % 2):64 * (h % 2) + 64, q0:q0 + nq, h // 2, :],
                    start=True, stop=True),
                    waits=w, inc=("sT", 1), ev=fin)

            def emit_exp(h, ki, kt):
                qlo = max(jt0, kt)
                qhi = min(jt0 + 3, kt + dlim(h))
                nq = qhi - qlo + 1
                q0 = qlo - jt0
                pscn, psc = sc_bank(h, kt)
                fin = "qk_%d_%d_%d" % (g, h, kt)
                has_diag = jt0 <= kt <= jt0 + 3
                has_edge = dlim(h) == 4 and jt0 <= kt + 4 <= jt0 + 3
                blk = pb_blk(h, kt)
                pb = PBr[:, blk, :]
                expw = [fin] + pb_readers[blk]
                pb_readers[blk] = []
                if h in PV_HEADS:
                    for qt in range(qlo, qhi + 1):
                        dd = qt - kt
                        col = ((0 if kt < 4 else 1) * pvlen +
                               PV_COMBOS.index((h, dd)))
                        qo2 = qt - jt0
                        last = (qt == qhi)
                        ir.add("S", lambda s, psc=psc, pb=pb, qo2=qo2, col=col: s.activation(
                            pb[:, 128 * qo2:128 * (qo2 + 1)],
                            psc[:, 128 * qo2:128 * (qo2 + 1)],
                            AF.Exp, bias=BTV[:, col:col + 1],
                            scale=INV_SQRT_D),
                            waits=expw, inc=("sS", 1),
                            ev=("exps_%d_%d_%d" % (g, h, kt)) if last else None)
                        expw = []
                else:
                    opp = kt - 4 * g
                    gcase = 0 if g == 0 else 1
                    col = (gcase * 8 + opp) * NH + h
                    ir.add("S", lambda s, psc=psc, pb=pb, q0=q0, nq=nq, col=col: s.activation(
                        pb[:, 128 * q0:128 * (q0 + nq)],
                        psc[:, 128 * q0:128 * (q0 + nq)],
                        AF.Exp, bias=BTG[:, col:col + 1], scale=INV_SQRT_D),
                        waits=expw, inc=("sS", 1),
                        ev="exps_%d_%d_%d" % (g, h, kt))
                pread(pscn, "exps_%d_%d_%d" % (g, h, kt))
                mev = "exps_%d_%d_%d" % (g, h, kt)
                if has_diag or has_edge:
                    msk = MD if has_diag else ME
                    qo = (kt - jt0) if has_diag else (kt + 4 - jt0)
                    ir.add("V", lambda v, pb=pb, qo=qo, msk=msk: v.tensor_tensor(
                        pb[:, 128 * qo:128 * (qo + 1)],
                        pb[:, 128 * qo:128 * (qo + 1)],
                        msk[:, :], op=ALU.mult),
                        waits=[mev], inc=("sV", 1),
                        ev="exp_%d_%d_%d" % (g, h, kt))
                else:
                    ir.events["exp_%d_%d_%d" % (g, h, kt)] = ir.events[mev]

            def emit_av(h, kt):
                pavn = "PE" if (h % 2 == 0) else "PF2"
                pav = PS[pavn]
                qlo = max(jt0, kt)
                qhi = min(jt0 + 3, kt + dlim(h))
                nq = qhi - qlo + 1
                q0 = qlo - jt0
                pb = PBr[:, pb_blk(h, kt), :]
                avw = ["exp_%d_%d_%d" % (g, h, kt)]
                if not av_state.get(h):
                    avw += pwrite(pavn)
                    st_flag = True
                    av_state[h] = True
                else:
                    st_flag = False
                ave = "av_%d_%d_%d" % (g, h, kt)
                ir.add("T", lambda te, pav=pav, pb=pb, kt=kt, q0=q0, nq=nq, h=h,
                       st_=st_flag, sp_=(kt == gt0 + 3): te.matmul(
                    pav[0:65, 128 * q0:128 * (q0 + nq)],
                    VRr[:, kt % 8, h, 0:65],
                    pb[:, 128 * q0:128 * (q0 + nq)],
                    start=st_, stop=sp_),
                    waits=avw, inc=("sT", 1), ev=ave)
                pb_readers[pb_blk(h, kt)].append(ave)

            def emit_tail(h):
                pavn = "PE" if (h % 2 == 0) else "PF2"
                pav = PS[pavn]
                uw = ["av_%d_%d_%d" % (g, h, gt0 + 3)]
                if h <= 1 and g >= 1:
                    uw.append(("syu", 16 * 16 * g))
                ir.add("V", lambda v, h=h, pav=pav: v.tensor_copy(
                    UBr[0:64, h, :], pav[0:64, :]),
                    waits=uw, inc=("sV", 1), ev="uev_%d_%d" % (g, h))
                pread(pavn, "uev_%d_%d" % (g, h))
                lw = ["av_%d_%d_%d" % (g, h, gt0 + 3)]
                lk = "lgat_%d_%d" % (g, h - 2)
                if lk in ir.events:
                    lw.append(lk)
                elif g >= 1 and ("lgat_%d_%d" % (g - 1, h + HC - 2)) in ir.events:
                    lw.append("lgat_%d_%d" % (g - 1, h + HC - 2))
                ir.add("S", lambda s, h=h, pav=pav: s.copy(
                    LRr[64:65, h % 2, :], pav[64:65, :]),
                    waits=lw, inc=("sS", 1), ev="lev_%d_%d" % (g, h))
                pread(pavn, "lev_%d_%d" % (g, h))
                syw = ["lev_%d_%d" % (g, h)]
                if h == 0:
                    drain_t = (gt0 + 7) if (g + 1 < NGR) else tiles[3]
                    syw.append("ln1_tr_%d" % drain_t)
                    p2 = "ln2_tr_%d" % (gt0 - 1)
                    if p2 in ir.events:
                        syw.append(p2)
                    if g >= 1:
                        syw.append("recip_%d" % (g - 1))
                        syw.append(("syu", 16 * 16 * g))
                ir.add("SY", lambda sy, h=h: sy.dma_start(
                    out=LT[h:h + 1, :], in_=LRr[64:65, h % 2, :]),
                    waits=syw, inc=("syl%d" % (h % 2), 16),
                    ev="lgat_%d_%d" % (g, h))

            for hp in range(HC // 2):
                h0, h1 = 2 * hp, 2 * hp + 1
                kts0, kts1 = head_kts(h0), head_kts(h1)
                if h0 in PV_HEADS:
                    emit_zero(h0, first=(hp == 0))
                if h1 in PV_HEADS:
                    emit_zero(h1, first=False)
                n = max(len(kts0), len(kts1))
                sched = []
                for ki in range(n):
                    if ki < len(kts0):
                        sched.append((h0, ki, kts0[ki]))
                    if ki < len(kts1):
                        sched.append((h1, ki, kts1[ki]))
                pend = []
                for i, (h, ki, kt) in enumerate(sched):
                    emit_qk(h, ki, kt, first=(hp == 0 and i == 0 and
                                              h0 not in PV_HEADS))
                    blk = pb_blk(h, kt)
                    while pend and (len(pend) >= 4 or
                                    any(pb_blk(hh, kk) == blk for hh, kk in pend)):
                        hh, kk = pend.pop(0)
                        emit_av(hh, kk)
                    emit_exp(h, ki, kt)
                    pend.append((h, kt))
                for hh, kk in pend:
                    emit_av(hh, kk)
                emit_tail(h0)
                emit_tail(h1)
            ir.add("T", lambda te: te.nop(),
                   waits=["av_%d_%d_%d" % (g, HC - 1, gt0 + 3)],
                   inc=("sT", 1), ev="attn_done_%d" % g)

            rw = ["lgat_%d_%d" % (g, HC - 2 if HC >= 2 else 0),
                  "lgat_%d_%d" % (g, HC - 1)]
            pr15 = "rp_%d_%d" % (g - 1, 15)
            if pr15 in ir.events:
                rw.append(pr15)
            ir.add("V", lambda v: v.reciprocal(LT[:, :], LT[:, :]),
                   waits=rw, inc=("sV", 1), ev="recq_%d" % g)
            ir.add("V", lambda v: v.tensor_copy(LTR[:, :], LT[:, :]),
                   waits=["recq_%d" % g], inc=("sV", 1), ev="recip_%d" % g)

            for h in range(HC):
                prn = "PC" if (h % 2 == 0) else "PD"
                pr = PS[prn]
                w = pwrite(prn) + ["recip_%d" % g, "ega_done"]
                ir.add("T", lambda te, h=h, pr=pr: te.matmul(
                    pr[0:64, :], EGA[:, 64 * h:64 * (h + 1)], LTR[:, :],
                    start=True, stop=True),
                    waits=w, inc=("sT", 1), ev="rp_%d_%d" % (g, h))
                ir.add("V", lambda v, h=h, pr=pr: v.tensor_tensor(
                    UBr[0:64, h, :], UBr[0:64, h, :], pr[0:64, :],
                    op=ALU.mult),
                    waits=["rp_%d_%d" % (g, h), "uev_%d_%d" % (g, h)],
                    inc=("sV", 1), ev="umul_%d_%d" % (g, h))
                pread(prn, "umul_%d_%d" % (g, h))
                syw = ["umul_%d_%d" % (g, h)]
                if h == 0:
                    syw += ["lgat_%d_%d" % (g, HC - 2 if HC >= 2 else 0),
                            "lgat_%d_%d" % (g, HC - 1)]
                    if g >= 1:
                        syw.append("resid_%d_%d" % (g - 1, 3))
                ir.add("SY", lambda sy, h=h: sy.dma_start_transpose(
                    UTr[:, :, 64 * h:64 * (h + 1)], UBr[0:64, h, :]),
                    waits=syw, inc=("syu", 16), ev="utr_%d_%d" % (g, h))

            for ti, t in enumerate(tiles):
                s = t % 8
                ir.add("V", lambda v, s=s, ti=ti: v.tensor_tensor(
                    XRr[:, s, :], XRr[:, s, :], UTr[:, ti, :], op=ALU.add),
                    waits=["utr_%d_%d" % (g, HC - 1)],
                    inc=("sV", 1), ev="resid_%d_%d" % (g, ti))

            if PHASE < 3:
                for ti, t in enumerate(tiles):
                    s = t % 8
                    n_store += 1
                    ir.add("G", lambda g_, t=t, s=s: g_.dma_start(
                        out=yout[128 * (t - 4):128 * (t - 3), :], in_=XRr[:, s, :]),
                        waits=["resid_%d_%d" % (g, ti)], inc=("sst", 16))
                for t in range(gt0 + 8, min(gt0 + 12, NT)):
                    load_x(t, waits=[("sst", 16 * n_store)])
                continue
            def ln2_tile_ops(ti, sid=0, buf=None, guard=True):
                t = tiles[ti]
                s = t % 8
                extra_sy = []
                pk = "ffnmm_%d_%d" % (g, ti - 2)
                if pk in ir.events:
                    extra_sy.append(pk)
                else:
                    pk2 = "ffnmm_%d_%d" % (g - 1, ti + 2)
                    if pk2 in ir.events:
                        extra_sy.append(pk2)
                buf = X2N if buf is None else buf
                return (ln_ops(t, XRr[:, s, :], buf[:, :],
                               H2Fr[:, ti % 2, :, :], "syx2%d" % (t % 2),
                               "ln2", sid,
                               ["resid_%d_%d" % (g, ti)], extra_sy, guard) +
                        mod_ops(t, H2Fr[:, ti % 2, :, :], CBF, SBF,
                                "ln2_tr_%d" % t, "h2"))

            def emit_ln2(ti):
                for f in ln2_tile_ops(ti):
                    f()

            def emit_ffn(ti):
                t = tiles[ti]
                for k8 in range(8):
                    for half2 in range(2):
                        pfn = "PE" if half2 == 0 else "PF2"
                        pf_ = PS[pfn]
                        w = []
                        if k8 == 0 and half2 == 0:
                            w = (pwrite("PE") + pwrite("PF2") +
                                 ["h2_mod_%d" % t, "wf_scaled"])
                        ir.add("T", lambda te, ti=ti, k8=k8, half2=half2, pf_=pf_: te.matmul(
                            pf_[:, :],
                            H2Fr[:, ti % 2, k8, :],
                            WFr[:, k8, 512 * half2:512 * (half2 + 1)],
                            start=(k8 == 0), stop=(k8 == 7)),
                            waits=w,
                            inc=("sT", 1) if (k8 == 7 and half2 == 1) else None,
                            ev=("ffnmm_%d_%d" % (g, ti)) if (k8 == 7 and half2 == 1) else None)

            def emit_epi(ti):
                t = tiles[ti]
                s = t % 8
                FY = PBUF[:, 0:1024]
                w = (["ffnmm_%d_%d" % (g, ti), "biasr_done", "gbr_done"] +
                     pb_readers[0] + pb_readers[1])
                pb_readers[0] = []
                pb_readers[1] = []
                ir.add("V", lambda v, FY=FY: v.tensor_scalar(
                    FY[:, 0:512], PE[:, :], SSV[:, :], None, op0=ALU.mult),
                    waits=w, inc=("sV", 1), ev="y1a_%d_%d" % (g, ti))
                pread("PE", "y1a_%d_%d" % (g, ti))
                ir.add("S", lambda s_, FY=FY: s_.activation(
                    FY[:, 512:1024], PF2[:, :], AF.Copy,
                    bias=0.0, scale=SSV[:, 0:1]),
                    waits=list(w),
                    inc=("sS", 1), ev="y1_%d_%d" % (g, ti))
                pread("PF2", "y1_%d_%d" % (g, ti))
                ir.add("V", lambda v, FY=FY: v.tensor_tensor(
                    FY, FY, BIASR[:, :], op=ALU.add),
                    waits=["y1a_%d_%d" % (g, ti), "y1_%d_%d" % (g, ti)],
                    inc=("sV", 1), ev="y2_%d_%d" % (g, ti))
                y3w = ["y2_%d_%d" % (g, ti)]
                if ti == 0:
                    y3w += ps_readers["PB2a"] + ps_readers["PB2b"]
                    ps_readers["PB2a"] = []
                    ps_readers["PB2b"] = []
                ir.add("V", lambda v, FY=FY: v.tensor_tensor(
                    PB2[:, :], FY, GBR[:, :], op=ALU.mult),
                    waits=y3w, inc=("sV", 1), ev="y3_%d_%d" % (g, ti))
                ir.add("V", lambda v, s=s: v.tensor_tensor(
                    XRr[:, s, :], XRr[:, s, :], PB2[:, :], op=ALU.add),
                    waits=["y3_%d_%d" % (g, ti)],
                    inc=("sV", 1), ev="ffnep_%d_%d" % (g, ti))
                pb_readers[0].append("ffnep_%d_%d" % (g, ti))
                pb_readers[1].append("ffnep_%d_%d" % (g, ti))
                pread("PB2a", "ffnep_%d_%d" % (g, ti))
                pread("PB2b", "ffnep_%d_%d" % (g, ti))
                nonlocal_store(t, s, "ffnep_%d_%d" % (g, ti))

            def nonlocal_store(t, s, wev):
                ir.add("G", lambda g_, t=t, s=s: g_.dma_start(
                    out=yout[128 * (t - 4):128 * (t - 3), :], in_=XRr[:, s, :]),
                    waits=[wev], inc=("sst", 16))

            hasB = g + 1 < NGR
            hasA = g + 2 < NGR

            def load_ahead(ti):
                tl = gt0 + 8 + ti
                if tl < NT:
                    load_x(tl, waits=[("sst", 16 * n_store)])

            if hasB:
                emit_B_kq(g + 1, [0, 1, 2, 3, 4, 5])
            emit_zip(ln2_tile_ops(0),
                     ln2_tile_ops(1, sid=1, buf=XN, guard=False))
            emit_ffn(0)
            if hasB:
                emit_B_kq(g + 1, [6, 7, 8, 9, 10, 11])
            emit_epi(0)
            n_store += 1
            load_ahead(0)
            emit_ffn(1)
            if hasB:
                emit_B_kq(g + 1, [12, 13, 14, 15])
                emit_B_v(g + 1, [0])
            if hasA:
                emit_A_tile(g + 2, 0,
                            ["qkmm_%d_15" % (g + 1), "vmm_%d_0" % (g + 1)])
            emit_ln2(2)
            emit_epi(1)
            n_store += 1
            load_ahead(1)
            emit_ffn(2)
            if hasB:
                emit_B_v(g + 1, [1, 2])
            if hasA:
                emit_A_tile(g + 2, 1,
                            ["qkmm_%d_15" % (g + 1), "vmm_%d_1" % (g + 1)])
            emit_ln2(3)
            emit_epi(2)
            n_store += 1
            load_ahead(2)
            emit_ffn(3)
            if hasB:
                emit_B_v(g + 1, [3])
                emit_B_done(g + 1)
            if hasA:
                emit_A_tile(g + 2, 2,
                            ["qkmm_%d_15" % (g + 1), "vmm_%d_2" % (g + 1)])
            emit_epi(3)
            n_store += 1
            load_ahead(3)
            if hasA:
                emit_A_tile(g + 2, 3,
                            ["qkmm_%d_15" % (g + 1), "vmm_%d_3" % (g + 1)])

        ir.add("G", lambda g_: g_.nop(), waits=[("sst", 16 * n_store)],
               inc=("sg", 1))

        with nc.Block() as blk:
            def emit_engine(name, dec):
                ops = [(w, i, e) for (en, w, i, e) in ir.ops if en == name]

                def body(eng, ops=ops):
                    for waits, incs, emit in ops:
                        for semn, val in waits:
                            eng.wait_ge(sems[semn], val)
                        inst = emit(eng)
                        for semn, amt in incs:
                            inst.then_inc(sems[semn], amt)
                dec(body)

            emit_engine("T", blk.tensor)
            emit_engine("V", blk.vector)
            emit_engine("S", blk.scalar)
            emit_engine("SY", blk.sync)
            emit_engine("G", blk.gpsimd)

    return nc


_NC_CACHE = None


def _get_nc():
    global _NC_CACHE
    if _NC_CACHE is None:
        _NC_CACHE = build_nc()
    return _NC_CACHE


def kernel(L0_attn_x, L0_attn_w_kqv, L0_ffn_kernel, L0_ffn_input_scale,
           L0_ffn_kernel_scale, L0_ffn_bias, L0_c, L0_wmod):
    x = np.ascontiguousarray(np.asarray(L0_attn_x, dtype=np.float32))
    wkqv = np.ascontiguousarray(np.asarray(L0_attn_w_kqv, dtype=np.float32))
    wmod_a = np.ascontiguousarray(np.asarray(L0_wmod, dtype=np.float32))
    fker = np.ascontiguousarray(np.asarray(L0_ffn_kernel).astype(np.float32))
    fbias = np.ascontiguousarray(np.asarray(L0_ffn_bias).astype(np.float32)).reshape(1, D)
    sin = np.asarray(L0_ffn_input_scale, dtype=np.float32).reshape(1, 1)
    sk = np.asarray(L0_ffn_kernel_scale, dtype=np.float32).reshape(1, 1)
    cc = np.asarray(L0_c, dtype=np.float32)

    nc = _get_nc()
    in_maps = []
    zpad = np.zeros((512, D), dtype=np.float32)
    ident1 = np.ones((1, 1), dtype=np.float32)
    negi = np.diag(np.full(128, -1.0e5, dtype=np.float32))
    tabs = [make_tables(0), make_tables(1)]
    for c in range(8):
        b, half = c // 2, c % 2
        if half == 0:
            xs = np.concatenate([zpad, x[b, 0:2048]], axis=0)
        else:
            xs = np.ascontiguousarray(x[b, 1536:4096])
        cfm = np.ascontiguousarray(cc[b].reshape(8, 128).T)
        btg, btv, mdiag, medge = tabs[half]
        in_maps.append({
            "xin": xs, "wkqv": wkqv, "wmod": wmod_a, "fker": fker,
            "fbias": fbias, "sin": sin, "sk": sk, "cfm": cfm,
            "btg": btg, "btv": btv, "mdiag": mdiag, "medge": medge,
            "negi": negi, "ident1": ident1,
        })
    res = run_bass_kernel_spmd(nc, in_maps, core_ids=list(range(8)))
    out = np.empty((B, SEQ, D), dtype=np.float32)
    for c in range(8):
        b, half = c // 2, c % 2
        out[b, half * 2048:(half + 1) * 2048] = res.results[c]["yout"]
    return out




# version 24 (final)
# speedup vs baseline: 1.0134x; 1.0134x over previous
import numpy as np
import ml_dtypes
import concourse.bass as bass
import concourse.mybir as mybir
from concourse.bass_utils import run_bass_kernel_spmd

F32 = mybir.dt.float32
BF16 = mybir.dt.bfloat16
AF = mybir.ActivationFunctionType
ALU = mybir.AluOpType

B, SEQ, D = 4, 4096, 1024
NH, HD = 16, 64
NT = 20
NQT = 16
NGR = 4
CORE_ROWS = NT * 128
OUT_ROWS = NQT * 128
SLOPES = [2.0 ** (-(h + 1) / 2.0) for h in range(NH)]
_DLIM = {0: 1, 1: 2, 2: 2, 3: 3}


def dlim(h):
    return _DLIM.get(h, 4)


PV_HEADS = (0, 1, 2)
PV_COMBOS = [(h, d) for h in PV_HEADS for d in range(dlim(h) + 1)]
SHIFT_G = 256.0
SHIFT_V = 64.0
NEGB = -1.0e30
INV_SQRT_D = 1.0 / 32.0


def make_tables(half):
    kloc = np.arange(128, dtype=np.float64)
    btabG = np.zeros((128, 2 * 8 * NH), np.float32)
    for gcase in (0, 1):
        for op_ in range(8):
            for h in range(NH):
                col = (gcase * 8 + op_) * NH + h
                invalid = (half == 0) and (gcase == 0) and (op_ < 4)
                btabG[:, col] = NEGB if invalid else (
                    SLOPES[h] * (kloc + 128.0 * (op_ - 4) - SHIFT_G))
    btabV = np.zeros((128, 2 * len(PV_COMBOS)), np.float32)
    for case in range(2):
        for i, (h, d) in enumerate(PV_COMBOS):
            col = case * len(PV_COMBOS) + i
            invalid = (half == 0) and (case == 0)
            btabV[:, col] = NEGB if invalid else (
                SLOPES[h] * (kloc - 128.0 * d - SHIFT_V))
    k = np.arange(128)[:, None]
    q = np.arange(128)[None, :]
    mdiag = (k <= q).astype(np.float32)
    medge = (k >= q).astype(np.float32)
    return btabG, btabV, mdiag, medge


class IR:
    def __init__(self):
        self.ops = []
        self.ctr = {}
        self.events = {}

    def add(self, eng, emit, waits=(), inc=None, ev=None):
        rw = []
        for w in waits:
            if w is None:
                continue
            if isinstance(w, str):
                assert w in self.events, f"event {w} not yet defined"
                rw.append(self.events[w])
            else:
                rw.append(w)
        ri = []
        if inc is not None:
            sem, amt = inc
            self.ctr[sem] = self.ctr.get(sem, 0) + amt
            ri.append((sem, amt))
            if ev is not None:
                self.events[ev] = (sem, self.ctr[sem])
        else:
            assert ev is None
        self.ops.append((eng, rw, ri, emit))

    def cum(self, sem):
        return self.ctr.get(sem, 0)


def build_nc():
    import os
    PHASE = int(os.environ.get("KDBG_PHASE", "3"))
    HC = int(os.environ.get("KDBG_HC", "16"))
    nc = bass.Bass("TRN2", target_bir_lowering=False)

    dt = nc.dram_tensor
    xin = dt("xin", [CORE_ROWS, D], F32, kind="ExternalInput").ap()
    wkqv = dt("wkqv", [D, 3 * D], F32, kind="ExternalInput").ap()
    wmod = dt("wmod", [D, 6 * D], F32, kind="ExternalInput").ap()
    fker = dt("fker", [D, D], F32, kind="ExternalInput").ap()
    fbias = dt("fbias", [1, D], F32, kind="ExternalInput").ap()
    sin_d = dt("sin", [1, 1], F32, kind="ExternalInput").ap()
    sk_d = dt("sk", [1, 1], F32, kind="ExternalInput").ap()
    cfm_d = dt("cfm", [128, 8], F32, kind="ExternalInput").ap()
    btg_d = dt("btg", [128, 256], F32, kind="ExternalInput").ap()
    btv_d = dt("btv", [128, 2 * len(PV_COMBOS)], F32, kind="ExternalInput").ap()
    md_d = dt("mdiag", [128, 128], F32, kind="ExternalInput").ap()
    me_d = dt("medge", [128, 128], F32, kind="ExternalInput").ap()
    ng_d = dt("negi", [128, 128], F32, kind="ExternalInput").ap()
    id_d = dt("ident1", [1, 1], F32, kind="ExternalInput").ap()
    yout = dt("yout", [OUT_ROWS, D], F32, kind="ExternalOutput").ap()

    import contextlib
    st = contextlib.ExitStack()
    sbt = lambda name, shape, dtp: st.enter_context(nc.sbuf_tensor(name, shape, dtp))
    pst = lambda name, shape: st.enter_context(nc.psum_tensor(name, shape, F32))

    with st:
        XR = sbt("XR", [128, 8 * 1024], F32)
        WQ = sbt("WQ", [128, 8 * 3072], BF16)
        WF = sbt("WF", [128, 8 * 1024], BF16)
        KR = sbt("KR", [128, 8 * 1024], BF16)
        VR = sbt("VR", [128, 8 * 1056], BF16)
        QG = sbt("QG", [128, 4 * 1024], BF16)
        HFM = sbt("HFM", [128, 4 * 1024], BF16)
        XN = sbt("XN", [128, 1024], BF16)
        X2N = sbt("X2N", [128, 1024], BF16)
        H2F = sbt("H2F", [128, 2 * 1024], BF16)
        PBUF = sbt("PBUF", [128, 4 * 512], BF16)
        ET = sbt("ET", [128, 128], BF16)
        ZB = sbt("ZB", [128, 128], BF16)
        IDB = sbt("IDB", [1, 1], BF16)
        MODF = sbt("MODF", [1, 128], F32)
        UB = sbt("UB", [128, 16 * 512], BF16)
        UT = sbt("UT", [128, 4 * 1024], BF16)
        LROW = sbt("LROW", [65, 2 * 512], F32)
        LT = sbt("LT", [16, 512], F32)
        LTR = sbt("LTR", [16, 512], BF16)
        MODROW = sbt("MODROW", [1, 1024], BF16)
        SCRA = sbt("SCRA", [1, 1024], F32)
        SCRB = sbt("SCRB", [1, 1024], F32)
        EGA = sbt("EGA", [16, 1024], BF16)
        GBR = sbt("GBR", [128, 1024], BF16)
        BIASR = sbt("BIASR", [128, 1024], BF16)
        CAF = sbt("CAF", [128, 8], F32)
        SAF = sbt("SAF", [128, 8], F32)
        CBF = sbt("CBF", [128, 8], F32)
        SBF = sbt("SBF", [128, 8], F32)
        SEF = sbt("SEF", [128, 8], F32)
        SCF = sbt("SCF", [128, 8], F32)
        SCB = sbt("SCB", [128, 8], BF16)
        BTG = sbt("BTG", [128, 256], F32)
        BTV = sbt("BTV", [128, 2 * len(PV_COMBOS)], F32)
        MD = sbt("MD", [128, 128], BF16)
        ME = sbt("ME", [128, 128], BF16)
        NG = sbt("NG", [128, 128], BF16)
        IDENT = sbt("IDENT", [1, 1], F32)
        SMALL = sbt("SMALL", [1, 8], F32)
        SSV = sbt("SSV", [128, 1], F32)
        RSIN = sbt("RSIN", [128, 1], F32)
        RSK = sbt("RSK", [128, 1], F32)
        BST = sbt("BST", [128, 12], F32)
        MV = sbt("MV", [128, 2], F32)
        LNT = sbt("LNT", [128, 1], F32)
        RV = sbt("RV", [128, 1], F32)
        NMR = sbt("NMR", [128, 1], F32)
        BST2 = sbt("BST2", [128, 12], F32)
        MV2 = sbt("MV2", [128, 2], F32)
        LNT2 = sbt("LNT2", [128, 1], F32)
        RV2 = sbt("RV2", [128, 1], F32)
        NMR2 = sbt("NMR2", [128, 1], F32)
        LNSCR = ((BST, MV, LNT, RV, NMR), (BST2, MV2, LNT2, RV2, NMR2))
        EPSV = sbt("EPSV", [128, 1], F32)
        ONES1 = sbt("ONES1", [1, 128], F32)

        PS = {"PA": pst("PA", [128, 1024]),
              "PB2": pst("PB2", [128, 1024]),
              "PC": pst("PC", [128, 512]),
              "PD": pst("PD", [128, 512]),
              "PE": pst("PE", [128, 512]),
              "PF2": pst("PF2", [128, 512])}
        PA, PB2, PC, PD, PE, PF2 = (PS[k] for k in
                                    ("PA", "PB2", "PC", "PD", "PE", "PF2"))

        sem_names = (["sx%d" % i for i in range(8)] +
                     ["sw", "swm0", "swm1", "sst", "sT", "sV", "sS",
                      "syxn0", "syxn1", "syxna0", "syxna1",
                      "syu", "syl0", "syl1",
                      "syx20", "syx21", "syx2a0", "syx2a1", "symisc", "sg"])
        sems = {n: st.enter_context(nc.semaphore(n)) for n in sem_names}

        ir = IR()
        ps_readers = {k: [] for k in PS}
        ps_readers["PA0"] = []
        ps_readers["PA1"] = []
        ps_readers["PB2a"] = []
        ps_readers["PB2b"] = []
        del ps_readers["PA"]
        del ps_readers["PB2"]

        def pwrite(name):
            ws = ps_readers[name]
            ps_readers[name] = []
            return list(ws)

        def pread(name, evname):
            if evname is not None:
                ps_readers[name].append(evname)

        pb_readers = {i: [] for i in range(4)}
        et_last = [None]

        WQr = WQ.rearrange("p (k n) -> p k n", k=8)
        WFr = WF.rearrange("p (k n) -> p k n", k=8)
        KRr = KR.rearrange("p (s j r) -> p s j r", s=8, j=8)
        VRr = VR.rearrange("p (s h c) -> p s h c", s=8, h=16)
        QGr = QG.rearrange("p (t j r) -> p t j r", t=4, j=8)
        HFMr = HFM.rearrange("p (t j r) -> p t j r", t=4, j=8)
        XRr = XR.rearrange("p (s n) -> p s n", s=8)
        H2Fr = H2F.rearrange("p (b j r) -> p b j r", b=2, j=8)
        PBr = PBUF.rearrange("p (b n) -> p b n", b=4)
        UBr = UB.rearrange("p (h n) -> p h n", h=16)
        UTr = UT.rearrange("p (t c) -> p t c", t=4)
        LRr = LROW.rearrange("p (b n) -> p b n", b=2)

        xld_count = [0] * 8

        def load_x(t, waits=()):
            s = t % 8
            xld_count[s] += 1
            ir.add("G", lambda g, t=t, s=s: g.dma_start(
                out=XRr[:, s, :], in_=xin[128 * t:128 * (t + 1), :]),
                waits=waits, inc=("sx%d" % s, 16))

        for t in range(8):
            load_x(t)
        for k8 in range(8):
            ir.add("G", lambda g, k8=k8: g.dma_start(
                out=WQr[:, k8, :], in_=wkqv[128 * k8:128 * (k8 + 1), :]),
                inc=("sw", 16))
        for k8 in range(8):
            ir.add("G", lambda g, k8=k8: g.dma_start(
                out=WFr[:, k8, :], in_=fker[128 * k8:128 * (k8 + 1), :]),
                inc=("sw", 16))
        for src, dst in ((cfm_d, SCF), (btg_d, BTG), (btv_d, BTV),
                         (md_d, MD), (me_d, ME), (ng_d, NG), (id_d, IDENT)):
            ir.add("G", lambda g, src=src, dst=dst: g.dma_start(
                out=dst[:, :], in_=src), inc=("sw", 16))
        ir.add("G", lambda g: g.dma_start(out=SMALL[0:1, 0:1], in_=sin_d),
               inc=("sw", 16))
        ir.add("G", lambda g: g.dma_start(out=SMALL[0:1, 1:2], in_=sk_d),
               inc=("sw", 16))
        ir.add("G", lambda g: g.dma_start(out=SCRB[0:1, :], in_=fbias),
               inc=("sw", 16), ev="fb_row")
        w_all = ("sw", ir.cum("sw"))

        ir.add("V", lambda v: v.memset(VRr[:, :, :, 64:65], 1.0), inc=("sV", 1))
        ir.add("V", lambda v: v.memset(EPSV[:, :], 1e-5), inc=("sV", 1), ev="eps_set")
        ir.add("V", lambda v: v.memset(ONES1[0:1, :], 1.0), inc=("sV", 1), ev="ones_set")
        ir.add("V", lambda v: v.memset(ZB[:, :], 0.0), inc=("sV", 1), ev="zb_set")
        ir.add("V", lambda v: v.memset(IDB[0:1, :], 1.0), inc=("sV", 1), ev="idb_set")
        ir.add("S", lambda s: s.nop(), inc=("sS", 1))


        def replicate_row(src_row, dst, n, waits, evname):
            nmm = (n + 511) // 512
            for i in range(nmm):
                w = list(waits) + ["ones_set"] + (pwrite("PF2") if i == 0
                                                  else ["%s_e%d" % (evname, i - 1)])
                c0, c1 = 512 * i, min(n, 512 * (i + 1))
                ir.add("T", lambda te, c0=c0, c1=c1: te.matmul(
                    PF2[:, 0:c1 - c0] if False else PF2[:, c0 % 512:(c0 % 512) + (c1 - c0)],
                    ONES1[0:1, :], src_row[0:1, c0:c1],
                    start=True, stop=True),
                    waits=w, inc=("sT", 1),
                    ev="%s_mm%d" % (evname, i))
                ir.add("V", lambda v, c0=c0, c1=c1: v.tensor_copy(
                    dst[:, c0:c1], PF2[:, c0 % 512:(c0 % 512) + (c1 - c0)]),
                    waits=["%s_mm%d" % (evname, i)], inc=("sV", 1),
                    ev=evname if i == nmm - 1 else ("%s_e%d" % (evname, i)))
                pread("PF2", evname if i == nmm - 1 else ("%s_e%d" % (evname, i)))

        ir.add("S", lambda s: s.activation(SEF[:, :], SCF[:, :], AF.Exp,
                                           bias=0.0, scale=-1.0),
               waits=[w_all], inc=("sS", 1), ev="silu_e")
        ir.add("V", lambda v: v.tensor_scalar(SEF[:, :], SEF[:, :], 1.0, None,
                                              op0=ALU.add),
               waits=["silu_e"], inc=("sV", 1), ev="sil1")
        ir.add("V", lambda v: v.reciprocal(SEF[:, :], SEF[:, :]),
               waits=["sil1"], inc=("sV", 1), ev="sil2")
        ir.add("V", lambda v: v.tensor_tensor(SCF[:, :], SCF[:, :], SEF[:, :],
                                              op=ALU.mult),
               waits=["sil2"], inc=("sV", 1), ev="sil3")
        ir.add("V", lambda v: v.tensor_copy(SCB[:, :], SCF[:, :]),
               waits=["sil3"], inc=("sV", 1), ev="silu_done")
        ir.add("V", lambda v: v.reciprocal(SMALL[0:1, 2:3], SMALL[0:1, 0:1]),
               waits=[w_all], inc=("sV", 1))
        ir.add("V", lambda v: v.reciprocal(SMALL[0:1, 3:4], SMALL[0:1, 1:2]),
               inc=("sV", 1))
        ir.add("V", lambda v: v.tensor_tensor(SMALL[0:1, 4:5], SMALL[0:1, 0:1],
                                              SMALL[0:1, 1:2], op=ALU.mult),
               inc=("sV", 1), ev="small_done")
        replicate_row(SMALL[0:1, 2:3], RSIN, 1, ["small_done"], "rsin_rep")
        replicate_row(SMALL[0:1, 3:4], RSK, 1, ["small_done"], "rsk_rep")
        replicate_row(SMALL[0:1, 4:5], SSV, 1, ["small_done"], "bcast_done")
        replicate_row(SCRB, BIASR, 1024, ["fb_row"], "biasr_done")
        def emit_modsection():
            ir.add("V", lambda v: v.tensor_scalar(WF[:, :], WF[:, :], RSK[:, :],
                                                  None, op0=ALU.mult),
                   waits=["rsk_rep", w_all], inc=("sV", 1), ev="wf_scaled")

            vec_psums = [PA[0:1, 0:512], PA[0:1, 512:1024],
                         PB2[0:1, 0:512], PB2[0:1, 512:1024],
                         PC[0:1, 0:512], PD[0:1, 0:512]]

            def wmod_stage(ci):
                par = ci % 2
                return UB[:, 3072 * par:3072 * (par + 1)]

            for half in range(2):
                for k8 in range(8):
                    ci = half * 8 + k8
                    par = ci % 2
                    dw = [] if ci < 2 else ["mmv_%d" % (ci - 2)]
                    ir.add("G", lambda g, half=half, k8=k8, ci=ci: g.dma_start(
                        out=wmod_stage(ci),
                        in_=wmod[128 * k8:128 * (k8 + 1),
                                 3072 * half:3072 * (half + 1)]),
                        waits=dw, inc=("swm%d" % par, 16))
                    for j in range(6):
                        w = [("swm%d" % par, 16 * (ci // 2 + 1))]
                        if ci == 0 and j == 0:
                            w += ["silu_done", w_all]
                        last = (j == 5)
                        ir.add("T", lambda te, k8=k8, j=j, ci=ci: te.matmul(
                            vec_psums[j], SCB[:, k8:k8 + 1],
                            wmod_stage(ci)[:, 512 * j:512 * (j + 1)],
                            start=(k8 == 0), stop=(k8 == 7)),
                            waits=w, inc=("sT", 1) if last else None,
                            ev=("mmv_%d" % ci) if last else None)
                for vv in range(3):
                    vec = half * 3 + vv
                    evw = ["mmv_%d" % (half * 8 + 7)]
                    ir.add("V", lambda v, vv=vv: v.tensor_copy(
                        MODROW[0:1, 0:512], vec_psums[2 * vv]),
                        waits=evw, inc=("sV", 1))
                    ir.add("V", lambda v, vv=vv: v.tensor_copy(
                        MODROW[0:1, 512:1024], vec_psums[2 * vv + 1]),
                        inc=("sV", 1), ev="modrow_%d" % vec)
                    for kk in (("PA0", "PA1"), ("PB2a", "PB2b"), ("PC", "PD"))[vv]:
                        pread(kk, "modrow_%d" % vec)
                    if vec in (0, 1, 3, 4):
                        dstv = {0: SAF, 1: CAF, 3: SBF, 4: CBF}[vec]
                        for j in range(8):
                            wv = ["modrow_%d" % vec] if j == 0 else []
                            if j > 0:
                                wv.append("vtrj_%d_%d" % (vec, j - 1))
                            ir.add("V", lambda v, j=j: v.tensor_copy(
                                MODF[0:1, :], MODROW[0:1, 128 * j:128 * (j + 1)]),
                                waits=wv, inc=("sV", 1), ev="modf_%d_%d" % (vec, j))
                            w = ["modf_%d_%d" % (vec, j)]
                            if j == 0:
                                w += pwrite("PE")
                            ir.add("T", lambda te, j=j: te.transpose(
                                PE[:, j:j + 1], MODF[0:1, :],
                                IDENT[0:1, 0:1]),
                                waits=w, inc=("sT", 1),
                                ev="vtrj_%d_%d" % (vec, j) if j < 7 else ("vtr_%d" % vec))
                        ir.add("V", lambda v, dstv=dstv: v.tensor_copy(
                            dstv[:, :], PE[:, 0:8]),
                            waits=["vtr_%d" % vec], inc=("sV", 1),
                            ev="vfm_%d" % vec)
                        pread("PE", "vfm_%d" % vec)
                    else:
                        cw = ["modrow_%d" % vec, "biasr_done_mm1"]
                        if vec == 5:
                            cw.append("ega_done")
                        ir.add("V", lambda v: v.memset(SCRB[0:1, :], 0.0),
                               waits=cw, inc=("sV", 1), ev="cl_%d_-1" % vec)
                        cseq = [
                            lambda v: v.tensor_tensor(SCRA[0:1, :], MODROW[0:1, :],
                                                      SCRB[0:1, :], op=ALU.is_ge),
                            lambda v: v.tensor_scalar(SCRA[0:1, :], SCRA[0:1, :],
                                                      2.0, -1.0, op0=ALU.mult,
                                                      op1=ALU.add),
                            lambda v: v.tensor_scalar(SCRB[0:1, :], MODROW[0:1, :],
                                                      -1.0, None, op0=ALU.mult),
                            lambda v: v.tensor_tensor(SCRB[0:1, :], MODROW[0:1, :],
                                                      SCRB[0:1, :], op=ALU.max),
                            lambda v: v.tensor_scalar(SCRB[0:1, :], SCRB[0:1, :],
                                                      -0.25, None, op0=ALU.add),
                            lambda v: v.tensor_relu(SCRB[0:1, :], SCRB[0:1, :]),
                            lambda v: v.tensor_scalar(SCRB[0:1, :], SCRB[0:1, :],
                                                      0.25, None, op0=ALU.add),
                            lambda v: v.tensor_tensor(SCRA[0:1, :], SCRA[0:1, :],
                                                      SCRB[0:1, :], op=ALU.mult),
                        ]
                        for ci2, fn in enumerate(cseq):
                            ir.add("V", fn,
                                   waits=["cl_%d_%d" % (vec, ci2 - 1)],
                                   inc=("sV", 1),
                                   ev=("clamp_%d" % vec) if ci2 == len(cseq) - 1
                                   else "cl_%d_%d" % (vec, ci2))
                        if vec == 2:
                            ir.add("G", lambda g: g.memset(EGA[:, :], 0.0),
                                   inc=("sg", 1), ev="ega_ms")
                            for h in range(16):
                                ir.add("G", lambda g, h=h: g.dma_start(
                                    out=EGA[h:h + 1, 64 * h:64 * (h + 1)],
                                    in_=SCRA[0:1, 64 * h:64 * (h + 1)]),
                                    waits=(["clamp_2", "ega_ms", "ln1_tr_3"]
                                           if h == 0 else ()),
                                    inc=("symisc", 16),
                                    ev="ega_done" if h == 15 else None)
                        else:
                            replicate_row(SCRA, GBR, 1024, ["clamp_5"],
                                          "gbr_done")
            ir.add("V", lambda v: v.tensor_scalar(CAF[:, :], CAF[:, :], 1.0, None,
                                                  op0=ALU.add),
                   waits=["vfm_1"], inc=("sV", 1))
            ir.add("V", lambda v: v.tensor_scalar(CBF[:, :], CBF[:, :], 1.0,
                                                  RSIN[:, :], op0=ALU.add,
                                                  op1=ALU.mult),
                   waits=["vfm_4", "rsin_rep"], inc=("sV", 1))
            ir.add("V", lambda v: v.tensor_scalar(SBF[:, :], SBF[:, :], RSIN[:, :],
                                                  None, op0=ALU.mult),
                   waits=["vfm_3"], inc=("sV", 1), ev="mod_ready")

        def ln_ops(t, src_ap, dst_ap, fm_dst, syp, prefix, sid=0,
                   extra_v_waits=(), extra_sy_waits=(), guard=True):
            BST_, MV_, LNT_, RV_, NMR_ = LNSCR[sid]
            ops = []
            prev_tr = "%s_tr_%d" % (prefix, t - 1)
            xnw = [prev_tr] if (guard and prev_tr in ir.events) else []
            ops.append(lambda: ir.add(
                "V", lambda v: v.bn_stats(BST_[:, 0:6], src_ap[:, 0:512]),
                waits=list(extra_v_waits), inc=("sV", 1)))
            ops.append(lambda: ir.add(
                "V", lambda v: v.bn_stats(BST_[:, 6:12], src_ap[:, 512:1024]),
                inc=("sV", 1), ev="%s_bst_%d" % (prefix, t)))
            ops.append(lambda: ir.add(
                "V", lambda v: v.bn_aggr(MV_[:, :], BST_[:, :]),
                waits=["%s_bst_%d" % (prefix, t)],
                inc=("sV", 1), ev="%s_mv_%d" % (prefix, t)))
            ops.append(lambda: ir.add(
                "S", lambda s: s.activation(LNT_[:, :], MV_[:, 1:2], AF.Ln,
                                            bias=EPSV[:, :], scale=1.0),
                waits=["%s_mv_%d" % (prefix, t), "eps_set"], inc=("sS", 1),
                ev="%s_lnt_%d" % (prefix, t)))
            ops.append(lambda: ir.add(
                "S", lambda s: s.activation(RV_[:, :], LNT_[:, :], AF.Exp,
                                            bias=0.0, scale=-0.5),
                waits=["%s_lnt_%d" % (prefix, t)],
                inc=("sS", 1), ev="%s_rv_%d" % (prefix, t)))
            ops.append(lambda: ir.add(
                "V", lambda v: v.tensor_scalar(NMR_[:, :], MV_[:, 0:1],
                                               RV_[:, :], -1.0,
                                               op0=ALU.mult, op1=ALU.mult),
                waits=["%s_rv_%d" % (prefix, t)], inc=("sV", 1),
                ev="%s_nmr_%d" % (prefix, t)))
            prev_tra = "%s_tra_%d" % (prefix, t - 1)
            xnaw = [prev_tra] if (guard and prev_tra in ir.events) else []
            ops.append(lambda: ir.add(
                "V", lambda v: v.tensor_scalar(dst_ap[:, 0:512],
                                               src_ap[:, 0:512], RV_[:, :],
                                               NMR_[:, :], op0=ALU.mult,
                                               op1=ALU.add),
                waits=xnaw + ["%s_nmr_%d" % (prefix, t)],
                inc=("sV", 1), ev="%s_xna_%d" % (prefix, t)))
            ops.append(lambda: ir.add(
                "SY", lambda sy: sy.dma_start_transpose(
                    fm_dst[:, 0:4, :], dst_ap[:, 0:512]),
                waits=["%s_xna_%d" % (prefix, t)] + list(extra_sy_waits),
                inc=(syp[:-1] + "a" + syp[-1], 16),
                ev="%s_tra_%d" % (prefix, t)))
            ops.append(lambda: ir.add(
                "V", lambda v: v.tensor_scalar(dst_ap[:, 512:1024],
                                               src_ap[:, 512:1024], RV_[:, :],
                                               NMR_[:, :], op0=ALU.mult,
                                               op1=ALU.add),
                waits=xnw, inc=("sV", 1), ev="%s_xn_%d" % (prefix, t)))
            ops.append(lambda: ir.add(
                "SY", lambda sy: sy.dma_start_transpose(
                    fm_dst[:, 4:8, :], dst_ap[:, 512:1024]),
                waits=["%s_xn_%d" % (prefix, t)],
                inc=(syp, 16), ev="%s_tr_%d" % (prefix, t)))
            return ops

        def mod_ops(t, fm_view, cvec, svec, tr_ev, prefix):
            tra_ev = tr_ev.replace("_tr_", "_tra_")
            ops = []
            for j in range(8):
                if j == 0:
                    w = [tra_ev, "mod_ready"]
                elif j == 4:
                    w = [tr_ev]
                else:
                    w = ()
                ops.append(lambda j=j, w=w: ir.add(
                    "V", lambda v, j=j: v.tensor_scalar(
                        fm_view[:, j, :], fm_view[:, j, :],
                        cvec[:, j:j + 1], svec[:, j:j + 1],
                        op0=ALU.mult, op1=ALU.add),
                    waits=w,
                    inc=("sV", 1) if j == 7 else None,
                    ev=("%s_mod_%d" % (prefix, t)) if j == 7 else None))
            return ops

        def emit_zip(la, lb):
            for i in range(max(len(la), len(lb))):
                if i < len(la):
                    la[i]()
                if i < len(lb):
                    lb[i]()
        def ln_pipeline(t, src_ap, dst_ap, fm_dst, syp, prefix,
                        extra_v_waits=(), extra_sy_waits=()):
            for f in ln_ops(t, src_ap, dst_ap, fm_dst, syp, prefix, 0,
                            extra_v_waits, extra_sy_waits):
                f()

        def modulate(t, fm_view, cvec, svec, tr_ev, prefix):
            for f in mod_ops(t, fm_view, cvec, svec, tr_ev, prefix):
                f()

        n_store = 0

        def a_tile_ops(gg, ti, extra_sy, sid=0):
            gt0a = 4 * (gg + 1)
            t = gt0a + ti
            s = t % 8
            xw = [("sx%d" % s, 16 * xld_count[s])]
            esy = list(extra_sy)
            if t == 4:
                esy.append("ega_done")
            return (ln_ops(t, XRr[:, s, :], XN[:, :],
                           HFMr[:, ti, :, :], "syxn%d" % (t % 2), "ln1", sid,
                           xw, esy) +
                    mod_ops(t, HFMr[:, ti, :, :], CAF, SAF,
                            "ln1_tr_%d" % t, "h1"))

        def emit_A_ln(gg, ti, extra_sy):
            gt0a = 4 * (gg + 1)
            t = gt0a + ti
            s = t % 8
            xw = [("sx%d" % s, 16 * xld_count[s])]
            esy = list(extra_sy)
            if t == 4:
                esy.append("ega_done")
            ln_pipeline(t, XRr[:, s, :], XN[:, :],
                        HFMr[:, ti, :, :], "syxn%d" % (t % 2), "ln1",
                        extra_v_waits=xw, extra_sy_waits=esy)

        def emit_A_tile(gg, ti, extra_sy):
            for f in a_tile_ops(gg, ti, extra_sy):
                f()

        def emit_A(gg):
            gt0a = 4 * (gg + 1)
            extra_sy = ["Bdone_%d" % (gg - 1)] if gg >= 0 else []
            for ti in range(4):
                emit_A_ln(gg, ti, extra_sy)
            for ti in range(4):
                t = gt0a + ti
                modulate(t, HFMr[:, ti, :, :], CAF, SAF, "ln1_tr_%d" % t, "h1")

        def emit_B_kq(gb, mis):
            gt0b = 4 * (gb + 1)
            tilesb = list(range(gt0b, gt0b + 4))
            hfm_ready = ["h1_mod_%d" % t for t in tilesb]
            for m in mis:
                pgn = "PA%d" % (m % 2)
                pg = PA[:, 0:512] if (m % 2 == 0) else PA[:, 512:1024]
                for k8 in range(8):
                    w = []
                    if k8 == 0:
                        w = pwrite(pgn)
                        if m == 0:
                            w += hfm_ready + [w_all]
                    ir.add("T", lambda te, m=m, k8=k8, pg=pg: te.matmul(
                        pg, WQr[:, k8, 128 * m:128 * (m + 1)],
                        HFMr[:, :, k8, :], start=(k8 == 0), stop=(k8 == 7)),
                        waits=w, inc=("sT", 1) if k8 == 7 else None,
                        ev=("qkmm_%d_%d" % (gb, m)) if k8 == 7 else None)
                if m < 8:
                    dst = KRr[:, (gt0b % 8):(gt0b % 8) + 4, m, :]
                else:
                    dst = QGr[:, :, m - 8, :]
                evw = ["qkmm_%d_%d" % (gb, m)]
                if m == 0 and gb >= 1 and ("attn_done_%d" % (gb - 1)) in ir.events:
                    evw.append("attn_done_%d" % (gb - 1))
                ir.add("S", lambda s_, dst=dst, pg=pg: s_.copy(
                    dst, pg.rearrange("p (t r) -> p t r", t=4)),
                    waits=evw, inc=("sS", 1),
                    ev="qkev_%d_%d" % (gb, m))
                pread(pgn, "qkev_%d_%d" % (gb, m))

        def emit_B_v(gb, tis):
            gt0b = 4 * (gb + 1)
            tilesb = list(range(gt0b, gt0b + 4))
            nm = 8 if gb < 0 else 16
            for ti in tis:
                t = tilesb[ti]
                for k8 in range(8):
                    for half2 in range(2):
                        pvn = "PC" if half2 == 0 else "PD"
                        pv_ = PS[pvn]
                        w = []
                        if k8 == 0:
                            w = pwrite(pvn)
                            if half2 == 0 and ti == 0:
                                w.append("qkmm_%d_%d" % (gb, nm - 1))
                        ir.add("T", lambda te, ti=ti, k8=k8, half2=half2, pv_=pv_: te.matmul(
                            pv_[:, :],
                            HFMr[:, ti, k8, :],
                            WQr[:, k8, 2048 + 512 * half2:2048 + 512 * (half2 + 1)],
                            start=(k8 == 0), stop=(k8 == 7)),
                            waits=w,
                            inc=("sT", 1) if (k8 == 7 and half2 == 1) else None,
                            ev=("vmm_%d_%d" % (gb, ti)) if (k8 == 7 and half2 == 1) else None)
                s = t % 8
                evw = ["vmm_%d_%d" % (gb, ti)]
                if ti == 0 and gb >= 1 and ("attn_done_%d" % (gb - 1)) in ir.events:
                    evw.append("attn_done_%d" % (gb - 1))
                ir.add("S", lambda s_, s=s: s_.copy(
                    VRr[:, s, 0:8, 0:64],
                    PC.rearrange("p (h c) -> p h c", h=8)),
                    waits=evw, inc=("sS", 1))
                pread("PC", None)
                ir.add("S", lambda s_, s=s: s_.copy(
                    VRr[:, s, 8:16, 0:64],
                    PD.rearrange("p (h c) -> p h c", h=8)),
                    inc=("sS", 1), ev="vev_%d_%d" % (gb, ti))
                pread("PC", "vev_%d_%d" % (gb, ti))
                pread("PD", "vev_%d_%d" % (gb, ti))

        def emit_B_done(gb):
            nm = 8 if gb < 0 else 16
            ir.add("T", lambda te: te.nop(),
                   waits=["qkev_%d_%d" % (gb, nm - 1), "vev_%d_%d" % (gb, 3)],
                   inc=("sT", 1), ev="Bdone_%d" % gb)

        def emit_B(gb):
            nm = 8 if gb < 0 else 16
            emit_B_kq(gb, list(range(nm)))
            emit_B_v(gb, [0, 1, 2, 3])
            emit_B_done(gb)

        for ti_ in range(4):
            emit_A_ln(-1, ti_, [])
        emit_modsection()
        for ti_ in range(4):
            modulate(ti_, HFMr[:, ti_, :, :], CAF, SAF,
                     "ln1_tr_%d" % ti_, "h1")
        emit_B(-1)
        for t in range(8, 12):
            load_x(t, waits=["ln1_xn_%d" % (t - 8)])
        emit_A(0)
        emit_B(0)
        emit_A(1)

        for g in range(0, NGR):
            gt0 = 4 * (g + 1)
            tiles = list(range(gt0, gt0 + 4))
            jt0 = gt0
            last_qkev = "qkev_%d_%d" % (g, 15)


            if PHASE < 2:
                for ti, t in enumerate(tiles):
                    s = t % 8
                    n_store += 1
                    ir.add("G", lambda g_, t=t, s=s: g_.dma_start(
                        out=yout[128 * (t - 4):128 * (t - 3), :], in_=XRr[:, s, :]),
                        waits=["ln1_xn_%d" % t, "vev_%d_%d" % (g, 3)],
                        inc=("sst", 16))
                for t in range(gt0 + 8, min(gt0 + 12, NT)):
                    load_x(t, waits=[("sst", 16 * n_store)])
                continue
            pvlen = len(PV_COMBOS)

            def sc_bank(h, kt):
                i = (kt % 2) * 2 + (h % 2)
                return (("PC", PS["PC"][:, :]), ("PD", PS["PD"][:, :]),
                        ("PB2a", PB2[:, 0:512]), ("PB2b", PB2[:, 512:1024]))[i]

            def pb_blk(h, kt):
                return (kt % 2) * 2 + (h % 2)

            def head_kts(h):
                if h in PV_HEADS:
                    return list(range(gt0 - dlim(h), gt0 + 4))
                if dlim(h) == 4:
                    return ([gt0, gt0 - 3, gt0 - 2, gt0 - 1, gt0 - 4] +
                            list(range(gt0 + 1, gt0 + 4)))
                return ([gt0] + list(range(gt0 - dlim(h), gt0)) +
                        list(range(gt0 + 1, gt0 + 4)))

            av_state = {}

            def emit_zero(h, first):
                pavn = "PE" if (h % 2 == 0) else "PF2"
                pav = PS[pavn]
                zw = ["zb_set"] + pwrite(pavn)
                if first:
                    zw += [last_qkev, "vev_%d_%d" % (g, 3)]
                ir.add("T", lambda te, pav=pav, h=h, gt0=gt0: te.matmul(
                    pav[0:65, 0:512], VRr[:, gt0 % 8, h, 0:65],
                    ZB[:, :].unsqueeze(1).broadcast_to([128, 4, 128]),
                    start=True, stop=False),
                    waits=zw, inc=("sT", 1))
                av_state[h] = True

            def emit_qk(h, ki, kt, first):
                qlo = max(jt0, kt)
                qhi = min(jt0 + 3, kt + dlim(h))
                nq = qhi - qlo + 1
                q0 = qlo - jt0
                pscn, psc = sc_bank(h, kt)
                w = pwrite(pscn)
                if first:
                    w += [last_qkev, "vev_%d_%d" % (g, 3)]
                fin = "qk_%d_%d_%d" % (g, h, kt)
                ir.add("T", lambda te, h=h, kt=kt, q0=q0, nq=nq, psc=psc: te.matmul(
                    psc[:, 128 * q0:128 * (q0 + nq)],
                    KRr[64 * (h % 2):64 * (h % 2) + 64, kt % 8, h // 2, :],
                    QGr[64 * (h % 2):64 * (h % 2) + 64, q0:q0 + nq, h // 2, :],
                    start=True, stop=True),
                    waits=w, inc=("sT", 1), ev=fin)

            def emit_exp(h, ki, kt):
                qlo = max(jt0, kt)
                qhi = min(jt0 + 3, kt + dlim(h))
                nq = qhi - qlo + 1
                q0 = qlo - jt0
                pscn, psc = sc_bank(h, kt)
                fin = "qk_%d_%d_%d" % (g, h, kt)
                has_diag = jt0 <= kt <= jt0 + 3
                has_edge = dlim(h) == 4 and jt0 <= kt + 4 <= jt0 + 3
                blk = pb_blk(h, kt)
                pb = PBr[:, blk, :]
                expw = [fin] + pb_readers[blk]
                pb_readers[blk] = []
                if h in PV_HEADS:
                    for qt in range(qlo, qhi + 1):
                        dd = qt - kt
                        col = ((0 if kt < 4 else 1) * pvlen +
                               PV_COMBOS.index((h, dd)))
                        qo2 = qt - jt0
                        last = (qt == qhi)
                        ir.add("S", lambda s, psc=psc, pb=pb, qo2=qo2, col=col: s.activation(
                            pb[:, 128 * qo2:128 * (qo2 + 1)],
                            psc[:, 128 * qo2:128 * (qo2 + 1)],
                            AF.Exp, bias=BTV[:, col:col + 1],
                            scale=INV_SQRT_D),
                            waits=expw, inc=("sS", 1),
                            ev=("exps_%d_%d_%d" % (g, h, kt)) if last else None)
                        expw = []
                else:
                    opp = kt - 4 * g
                    gcase = 0 if g == 0 else 1
                    col = (gcase * 8 + opp) * NH + h
                    ir.add("S", lambda s, psc=psc, pb=pb, q0=q0, nq=nq, col=col: s.activation(
                        pb[:, 128 * q0:128 * (q0 + nq)],
                        psc[:, 128 * q0:128 * (q0 + nq)],
                        AF.Exp, bias=BTG[:, col:col + 1], scale=INV_SQRT_D),
                        waits=expw, inc=("sS", 1),
                        ev="exps_%d_%d_%d" % (g, h, kt))
                pread(pscn, "exps_%d_%d_%d" % (g, h, kt))
                mev = "exps_%d_%d_%d" % (g, h, kt)
                if has_diag or has_edge:
                    msk = MD if has_diag else ME
                    qo = (kt - jt0) if has_diag else (kt + 4 - jt0)
                    ir.add("V", lambda v, pb=pb, qo=qo, msk=msk: v.tensor_tensor(
                        pb[:, 128 * qo:128 * (qo + 1)],
                        pb[:, 128 * qo:128 * (qo + 1)],
                        msk[:, :], op=ALU.mult),
                        waits=[mev], inc=("sV", 1),
                        ev="exp_%d_%d_%d" % (g, h, kt))
                else:
                    ir.events["exp_%d_%d_%d" % (g, h, kt)] = ir.events[mev]

            def emit_av(h, kt):
                pavn = "PE" if (h % 2 == 0) else "PF2"
                pav = PS[pavn]
                qlo = max(jt0, kt)
                qhi = min(jt0 + 3, kt + dlim(h))
                nq = qhi - qlo + 1
                q0 = qlo - jt0
                pb = PBr[:, pb_blk(h, kt), :]
                avw = ["exp_%d_%d_%d" % (g, h, kt)]
                if not av_state.get(h):
                    avw += pwrite(pavn)
                    st_flag = True
                    av_state[h] = True
                else:
                    st_flag = False
                ave = "av_%d_%d_%d" % (g, h, kt)
                ir.add("T", lambda te, pav=pav, pb=pb, kt=kt, q0=q0, nq=nq, h=h,
                       st_=st_flag, sp_=(kt == gt0 + 3): te.matmul(
                    pav[0:65, 128 * q0:128 * (q0 + nq)],
                    VRr[:, kt % 8, h, 0:65],
                    pb[:, 128 * q0:128 * (q0 + nq)],
                    start=st_, stop=sp_),
                    waits=avw, inc=("sT", 1), ev=ave)
                pb_readers[pb_blk(h, kt)].append(ave)

            def emit_tail(h):
                pavn = "PE" if (h % 2 == 0) else "PF2"
                pav = PS[pavn]
                uw = ["av_%d_%d_%d" % (g, h, gt0 + 3)]
                if h <= 1 and g >= 1:
                    uw.append(("syu", 16 * 16 * g))
                ir.add("V", lambda v, h=h, pav=pav: v.tensor_copy(
                    UBr[0:64, h, :], pav[0:64, :]),
                    waits=uw, inc=("sV", 1), ev="uev_%d_%d" % (g, h))
                pread(pavn, "uev_%d_%d" % (g, h))
                lw = ["av_%d_%d_%d" % (g, h, gt0 + 3)]
                lk = "lgat_%d_%d" % (g, h - 2)
                if lk in ir.events:
                    lw.append(lk)
                elif g >= 1 and ("lgat_%d_%d" % (g - 1, h + HC - 2)) in ir.events:
                    lw.append("lgat_%d_%d" % (g - 1, h + HC - 2))
                ir.add("S", lambda s, h=h, pav=pav: s.copy(
                    LRr[64:65, h % 2, :], pav[64:65, :]),
                    waits=lw, inc=("sS", 1), ev="lev_%d_%d" % (g, h))
                pread(pavn, "lev_%d_%d" % (g, h))
                syw = ["lev_%d_%d" % (g, h)]
                if h == 0:
                    drain_t = (gt0 + 7) if (g + 1 < NGR) else tiles[3]
                    syw.append("ln1_tr_%d" % drain_t)
                    p2 = "ln2_tr_%d" % (gt0 - 1)
                    if p2 in ir.events:
                        syw.append(p2)
                    if g >= 1:
                        syw.append("recip_%d" % (g - 1))
                        syw.append(("syu", 16 * 16 * g))
                ir.add("SY", lambda sy, h=h: sy.dma_start(
                    out=LT[h:h + 1, :], in_=LRr[64:65, h % 2, :]),
                    waits=syw, inc=("syl%d" % (h % 2), 16),
                    ev="lgat_%d_%d" % (g, h))

            for hp in range(HC // 2):
                h0, h1 = 2 * hp, 2 * hp + 1
                kts0, kts1 = head_kts(h0), head_kts(h1)
                if h0 in PV_HEADS:
                    emit_zero(h0, first=(hp == 0))
                if h1 in PV_HEADS:
                    emit_zero(h1, first=False)
                n = max(len(kts0), len(kts1))
                sched = []
                for ki in range(n):
                    if ki < len(kts0):
                        sched.append((h0, ki, kts0[ki]))
                    if ki < len(kts1):
                        sched.append((h1, ki, kts1[ki]))
                pend = []
                for i, (h, ki, kt) in enumerate(sched):
                    emit_qk(h, ki, kt, first=(hp == 0 and i == 0 and
                                              h0 not in PV_HEADS))
                    blk = pb_blk(h, kt)
                    while pend and (len(pend) >= 4 or
                                    any(pb_blk(hh, kk) == blk for hh, kk in pend)):
                        hh, kk = pend.pop(0)
                        emit_av(hh, kk)
                    emit_exp(h, ki, kt)
                    pend.append((h, kt))
                for hh, kk in pend:
                    emit_av(hh, kk)
                emit_tail(h0)
                emit_tail(h1)
            ir.add("T", lambda te: te.nop(),
                   waits=["av_%d_%d_%d" % (g, HC - 1, gt0 + 3)],
                   inc=("sT", 1), ev="attn_done_%d" % g)

            rw = ["lgat_%d_%d" % (g, HC - 2 if HC >= 2 else 0),
                  "lgat_%d_%d" % (g, HC - 1)]
            pr15 = "rp_%d_%d" % (g - 1, 15)
            if pr15 in ir.events:
                rw.append(pr15)
            ir.add("V", lambda v: v.reciprocal(LT[:, :], LT[:, :]),
                   waits=rw, inc=("sV", 1), ev="recq_%d" % g)
            ir.add("V", lambda v: v.tensor_copy(LTR[:, :], LT[:, :]),
                   waits=["recq_%d" % g], inc=("sV", 1), ev="recip_%d" % g)

            for h in range(HC):
                prn = "PC" if (h % 2 == 0) else "PD"
                pr = PS[prn]
                w = pwrite(prn) + ["recip_%d" % g, "ega_done"]
                ir.add("T", lambda te, h=h, pr=pr: te.matmul(
                    pr[0:64, :], EGA[:, 64 * h:64 * (h + 1)], LTR[:, :],
                    start=True, stop=True),
                    waits=w, inc=("sT", 1), ev="rp_%d_%d" % (g, h))
                ir.add("V", lambda v, h=h, pr=pr: v.tensor_tensor(
                    UBr[0:64, h, :], UBr[0:64, h, :], pr[0:64, :],
                    op=ALU.mult),
                    waits=["rp_%d_%d" % (g, h), "uev_%d_%d" % (g, h)],
                    inc=("sV", 1), ev="umul_%d_%d" % (g, h))
                pread(prn, "umul_%d_%d" % (g, h))
                syw = ["umul_%d_%d" % (g, h)]
                if h == 0:
                    syw += ["lgat_%d_%d" % (g, HC - 2 if HC >= 2 else 0),
                            "lgat_%d_%d" % (g, HC - 1)]
                    if g >= 1:
                        syw.append("resid_%d_%d" % (g - 1, 3))
                    fe = "ffnep_%d_3" % (g - 1)
                    if fe in ir.events:
                        syw.append(fe)
                ir.add("SY", lambda sy, h=h: sy.dma_start_transpose(
                    UTr[:, :, 64 * h:64 * (h + 1)], UBr[0:64, h, :]),
                    waits=syw, inc=("syu", 16), ev="utr_%d_%d" % (g, h))

            for ti, t in enumerate(tiles):
                s = t % 8
                ir.add("V", lambda v, s=s, ti=ti: v.tensor_tensor(
                    XRr[:, s, :], XRr[:, s, :], UTr[:, ti, :], op=ALU.add),
                    waits=["utr_%d_%d" % (g, HC - 1)],
                    inc=("sV", 1), ev="resid_%d_%d" % (g, ti))

            if PHASE < 3:
                for ti, t in enumerate(tiles):
                    s = t % 8
                    n_store += 1
                    ir.add("G", lambda g_, t=t, s=s: g_.dma_start(
                        out=yout[128 * (t - 4):128 * (t - 3), :], in_=XRr[:, s, :]),
                        waits=["resid_%d_%d" % (g, ti)], inc=("sst", 16))
                for t in range(gt0 + 8, min(gt0 + 12, NT)):
                    load_x(t, waits=[("sst", 16 * n_store)])
                continue
            def ln2_tile_ops(ti, sid=0, buf=None, guard=True):
                t = tiles[ti]
                s = t % 8
                extra_sy = []
                pk = "ffnmm_%d_%d" % (g, ti - 2)
                if pk in ir.events:
                    extra_sy.append(pk)
                else:
                    pk2 = "ffnmm_%d_%d" % (g - 1, ti + 2)
                    if pk2 in ir.events:
                        extra_sy.append(pk2)
                buf = X2N if buf is None else buf
                return (ln_ops(t, XRr[:, s, :], buf[:, :],
                               H2Fr[:, ti % 2, :, :], "syx2%d" % (t % 2),
                               "ln2", sid,
                               ["resid_%d_%d" % (g, ti)], extra_sy, guard) +
                        mod_ops(t, H2Fr[:, ti % 2, :, :], CBF, SBF,
                                "ln2_tr_%d" % t, "h2"))

            def emit_ln2(ti):
                for f in ln2_tile_ops(ti):
                    f()

            def emit_ffn(ti):
                t = tiles[ti]
                for k8 in range(8):
                    for half2 in range(2):
                        pfn = "PE" if half2 == 0 else "PF2"
                        pf_ = PS[pfn]
                        w = []
                        if k8 == 0 and half2 == 0:
                            w = (pwrite("PE") + pwrite("PF2") +
                                 ["h2_mod_%d" % t, "wf_scaled"])
                        ir.add("T", lambda te, ti=ti, k8=k8, half2=half2, pf_=pf_: te.matmul(
                            pf_[:, :],
                            H2Fr[:, ti % 2, k8, :],
                            WFr[:, k8, 512 * half2:512 * (half2 + 1)],
                            start=(k8 == 0), stop=(k8 == 7)),
                            waits=w,
                            inc=("sT", 1) if (k8 == 7 and half2 == 1) else None,
                            ev=("ffnmm_%d_%d" % (g, ti)) if (k8 == 7 and half2 == 1) else None)

            def emit_epi(ti):
                t = tiles[ti]
                s = t % 8
                FY = PBUF[:, 0:1024]
                w = (["ffnmm_%d_%d" % (g, ti), "biasr_done", "gbr_done"] +
                     pb_readers[0] + pb_readers[1])
                pb_readers[0] = []
                pb_readers[1] = []
                ir.add("V", lambda v, FY=FY: v.tensor_scalar(
                    FY[:, 0:512], PE[:, :], SSV[:, :], None, op0=ALU.mult),
                    waits=w, inc=("sV", 1), ev="y1a_%d_%d" % (g, ti))
                pread("PE", "y1a_%d_%d" % (g, ti))
                ir.add("S", lambda s_, FY=FY: s_.activation(
                    FY[:, 512:1024], PF2[:, :], AF.Copy,
                    bias=0.0, scale=SSV[:, 0:1]),
                    waits=list(w),
                    inc=("sS", 1), ev="y1_%d_%d" % (g, ti))
                pread("PF2", "y1_%d_%d" % (g, ti))
                ir.add("V", lambda v, FY=FY: v.tensor_tensor(
                    FY, FY, BIASR[:, :], op=ALU.add),
                    waits=["y1a_%d_%d" % (g, ti), "y1_%d_%d" % (g, ti)],
                    inc=("sV", 1), ev="y2_%d_%d" % (g, ti))
                ir.add("V", lambda v, FY=FY: v.tensor_tensor(
                    UT[:, 0:1024], FY, GBR[:, :], op=ALU.mult),
                    waits=["y2_%d_%d" % (g, ti)],
                    inc=("sV", 1), ev="y3_%d_%d" % (g, ti))
                ir.add("V", lambda v, s=s: v.tensor_tensor(
                    XRr[:, s, :], XRr[:, s, :], UT[:, 0:1024], op=ALU.add),
                    waits=["y3_%d_%d" % (g, ti)],
                    inc=("sV", 1), ev="ffnep_%d_%d" % (g, ti))
                pb_readers[0].append("ffnep_%d_%d" % (g, ti))
                pb_readers[1].append("ffnep_%d_%d" % (g, ti))
                nonlocal_store(t, s, "ffnep_%d_%d" % (g, ti))

            def nonlocal_store(t, s, wev):
                ir.add("G", lambda g_, t=t, s=s: g_.dma_start(
                    out=yout[128 * (t - 4):128 * (t - 3), :], in_=XRr[:, s, :]),
                    waits=[wev], inc=("sst", 16))

            hasB = g + 1 < NGR
            hasA = g + 2 < NGR

            def load_ahead(ti):
                tl = gt0 + 8 + ti
                if tl < NT:
                    load_x(tl, waits=[("sst", 16 * n_store)])

            if hasB:
                emit_B_kq(g + 1, [0, 1, 2, 3, 4, 5])
            emit_zip(ln2_tile_ops(0),
                     ln2_tile_ops(1, sid=1, buf=XN, guard=False))
            emit_ffn(0)
            if hasB:
                emit_B_kq(g + 1, [6, 7, 8, 9, 10, 11])
            emit_epi(0)
            n_store += 1
            load_ahead(0)
            emit_ffn(1)
            if hasB:
                emit_B_kq(g + 1, [12, 13, 14, 15])
                emit_B_v(g + 1, [0])
            if hasA:
                emit_A_tile(g + 2, 0,
                            ["qkmm_%d_15" % (g + 1), "vmm_%d_0" % (g + 1)])
                emit_ln2(2)
            else:
                emit_zip(ln2_tile_ops(2),
                         ln2_tile_ops(3, sid=1, buf=XN, guard=False))
            emit_epi(1)
            n_store += 1
            load_ahead(1)
            emit_ffn(2)
            if hasB:
                emit_B_v(g + 1, [1, 2])
            if hasA:
                emit_A_tile(g + 2, 1,
                            ["qkmm_%d_15" % (g + 1), "vmm_%d_1" % (g + 1)])
                emit_ln2(3)
            emit_epi(2)
            n_store += 1
            load_ahead(2)
            emit_ffn(3)
            if hasB:
                emit_B_v(g + 1, [3])
                emit_B_done(g + 1)
            if hasA:
                emit_A_tile(g + 2, 2,
                            ["qkmm_%d_15" % (g + 1), "vmm_%d_2" % (g + 1)])
            emit_epi(3)
            n_store += 1
            load_ahead(3)
            if hasA:
                emit_A_tile(g + 2, 3,
                            ["qkmm_%d_15" % (g + 1), "vmm_%d_3" % (g + 1)])

        ir.add("G", lambda g_: g_.nop(), waits=[("sst", 16 * n_store)],
               inc=("sg", 1))

        with nc.Block() as blk:
            def emit_engine(name, dec):
                ops = [(w, i, e) for (en, w, i, e) in ir.ops if en == name]

                def body(eng, ops=ops):
                    for waits, incs, emit in ops:
                        for semn, val in waits:
                            eng.wait_ge(sems[semn], val)
                        inst = emit(eng)
                        for semn, amt in incs:
                            inst.then_inc(sems[semn], amt)
                dec(body)

            emit_engine("T", blk.tensor)
            emit_engine("V", blk.vector)
            emit_engine("S", blk.scalar)
            emit_engine("SY", blk.sync)
            emit_engine("G", blk.gpsimd)

    return nc


_NC_CACHE = None


def _get_nc():
    global _NC_CACHE
    if _NC_CACHE is None:
        _NC_CACHE = build_nc()
    return _NC_CACHE


def kernel(L0_attn_x, L0_attn_w_kqv, L0_ffn_kernel, L0_ffn_input_scale,
           L0_ffn_kernel_scale, L0_ffn_bias, L0_c, L0_wmod):
    x = np.ascontiguousarray(np.asarray(L0_attn_x, dtype=np.float32))
    wkqv = np.ascontiguousarray(np.asarray(L0_attn_w_kqv, dtype=np.float32))
    wmod_a = np.ascontiguousarray(np.asarray(L0_wmod, dtype=np.float32))
    fker = np.ascontiguousarray(np.asarray(L0_ffn_kernel).astype(np.float32))
    fbias = np.ascontiguousarray(np.asarray(L0_ffn_bias).astype(np.float32)).reshape(1, D)
    sin = np.asarray(L0_ffn_input_scale, dtype=np.float32).reshape(1, 1)
    sk = np.asarray(L0_ffn_kernel_scale, dtype=np.float32).reshape(1, 1)
    cc = np.asarray(L0_c, dtype=np.float32)

    nc = _get_nc()
    in_maps = []
    zpad = np.zeros((512, D), dtype=np.float32)
    ident1 = np.ones((1, 1), dtype=np.float32)
    negi = np.diag(np.full(128, -1.0e5, dtype=np.float32))
    tabs = [make_tables(0), make_tables(1)]
    for c in range(8):
        b, half = c // 2, c % 2
        if half == 0:
            xs = np.concatenate([zpad, x[b, 0:2048]], axis=0)
        else:
            xs = np.ascontiguousarray(x[b, 1536:4096])
        cfm = np.ascontiguousarray(cc[b].reshape(8, 128).T)
        btg, btv, mdiag, medge = tabs[half]
        in_maps.append({
            "xin": xs, "wkqv": wkqv, "wmod": wmod_a, "fker": fker,
            "fbias": fbias, "sin": sin, "sk": sk, "cfm": cfm,
            "btg": btg, "btv": btv, "mdiag": mdiag, "medge": medge,
            "negi": negi, "ident1": ident1,
        })
    res = run_bass_kernel_spmd(nc, in_maps, core_ids=list(range(8)))
    out = np.empty((B, SEQ, D), dtype=np.float32)
    for c in range(8):
        b, half = c // 2, c % 2
        out[b, half * 2048:(half + 1) * 2048] = res.results[c]["yout"]
    return out




# version 25
# speedup vs baseline: 1.0174x; 1.0040x over previous
import numpy as np
import ml_dtypes
import concourse.bass as bass
import concourse.mybir as mybir
from concourse.bass_utils import run_bass_kernel_spmd

F32 = mybir.dt.float32
BF16 = mybir.dt.bfloat16
AF = mybir.ActivationFunctionType
ALU = mybir.AluOpType

B, SEQ, D = 4, 4096, 1024
NH, HD = 16, 64
NT = 20
NQT = 16
NGR = 4
CORE_ROWS = NT * 128
OUT_ROWS = NQT * 128
SLOPES = [2.0 ** (-(h + 1) / 2.0) for h in range(NH)]
_DLIM = {0: 1, 1: 2, 2: 2, 3: 3}


def dlim(h):
    return _DLIM.get(h, 4)


PV_HEADS = (0, 1, 2)
PV_COMBOS = [(h, d) for h in PV_HEADS for d in range(dlim(h) + 1)]
SHIFT_G = 256.0
SHIFT_V = 64.0
NEGB = -1.0e30
INV_SQRT_D = 1.0 / 32.0


def make_tables(half):
    kloc = np.arange(128, dtype=np.float64)
    btabG = np.zeros((128, 2 * 8 * NH), np.float32)
    for gcase in (0, 1):
        for op_ in range(8):
            for h in range(NH):
                col = (gcase * 8 + op_) * NH + h
                invalid = (half == 0) and (gcase == 0) and (op_ < 4)
                btabG[:, col] = NEGB if invalid else (
                    SLOPES[h] * (kloc + 128.0 * (op_ - 4) - SHIFT_G))
    btabV = np.zeros((128, 2 * len(PV_COMBOS)), np.float32)
    for case in range(2):
        for i, (h, d) in enumerate(PV_COMBOS):
            col = case * len(PV_COMBOS) + i
            invalid = (half == 0) and (case == 0)
            btabV[:, col] = NEGB if invalid else (
                SLOPES[h] * (kloc - 128.0 * d - SHIFT_V))
    k = np.arange(128)[:, None]
    q = np.arange(128)[None, :]
    mdiag = (k <= q).astype(np.float32)
    medge = (k >= q).astype(np.float32)
    return btabG, btabV, mdiag, medge


class IR:
    def __init__(self):
        self.ops = []
        self.ctr = {}
        self.events = {}

    def add(self, eng, emit, waits=(), inc=None, ev=None):
        rw = []
        for w in waits:
            if w is None:
                continue
            if isinstance(w, str):
                assert w in self.events, f"event {w} not yet defined"
                rw.append(self.events[w])
            else:
                rw.append(w)
        ri = []
        if inc is not None:
            sem, amt = inc
            self.ctr[sem] = self.ctr.get(sem, 0) + amt
            ri.append((sem, amt))
            if ev is not None:
                self.events[ev] = (sem, self.ctr[sem])
        else:
            assert ev is None
        self.ops.append((eng, rw, ri, emit))

    def cum(self, sem):
        return self.ctr.get(sem, 0)


def build_nc():
    import os
    PHASE = int(os.environ.get("KDBG_PHASE", "3"))
    HC = int(os.environ.get("KDBG_HC", "16"))
    nc = bass.Bass("TRN2", target_bir_lowering=False)

    dt = nc.dram_tensor
    xin = dt("xin", [CORE_ROWS, D], F32, kind="ExternalInput").ap()
    wkqv = dt("wkqv", [D, 3 * D], F32, kind="ExternalInput").ap()
    wmod = dt("wmod", [D, 6 * D], F32, kind="ExternalInput").ap()
    fker = dt("fker", [D, D], F32, kind="ExternalInput").ap()
    fbias = dt("fbias", [1, D], F32, kind="ExternalInput").ap()
    sin_d = dt("sin", [1, 1], F32, kind="ExternalInput").ap()
    sk_d = dt("sk", [1, 1], F32, kind="ExternalInput").ap()
    cfm_d = dt("cfm", [128, 8], F32, kind="ExternalInput").ap()
    btg_d = dt("btg", [128, 256], F32, kind="ExternalInput").ap()
    btv_d = dt("btv", [128, 2 * len(PV_COMBOS)], F32, kind="ExternalInput").ap()
    md_d = dt("mdiag", [128, 128], F32, kind="ExternalInput").ap()
    me_d = dt("medge", [128, 128], F32, kind="ExternalInput").ap()
    ng_d = dt("negi", [128, 128], F32, kind="ExternalInput").ap()
    id_d = dt("ident1", [1, 1], F32, kind="ExternalInput").ap()
    yout = dt("yout", [OUT_ROWS, D], F32, kind="ExternalOutput").ap()

    import contextlib
    st = contextlib.ExitStack()
    sbt = lambda name, shape, dtp: st.enter_context(nc.sbuf_tensor(name, shape, dtp))
    pst = lambda name, shape: st.enter_context(nc.psum_tensor(name, shape, F32))

    with st:
        XR = sbt("XR", [128, 8 * 1024], F32)
        WQ = sbt("WQ", [128, 8 * 3072], BF16)
        WF = sbt("WF", [128, 8 * 1024], BF16)
        KR = sbt("KR", [128, 8 * 1024], BF16)
        VR = sbt("VR", [128, 8 * 1056], BF16)
        QG = sbt("QG", [128, 4 * 1024], BF16)
        HFM = sbt("HFM", [128, 4 * 1024], BF16)
        XN = sbt("XN", [128, 1024], BF16)
        X2N = sbt("X2N", [128, 1024], BF16)
        H2F = sbt("H2F", [128, 2 * 1024], BF16)
        PBUF = sbt("PBUF", [128, 4 * 512], BF16)
        ET = sbt("ET", [128, 128], BF16)
        ZB = sbt("ZB", [128, 128], BF16)
        IDB = sbt("IDB", [1, 1], BF16)
        MODF = sbt("MODF", [1, 128], F32)
        UB = sbt("UB", [128, 16 * 512], BF16)
        UT = sbt("UT", [128, 4 * 1024], BF16)
        LROW = sbt("LROW", [65, 2 * 512], F32)
        LT = sbt("LT", [16, 512], F32)
        LTR = sbt("LTR", [16, 512], BF16)
        MODROW = sbt("MODROW", [1, 1024], BF16)
        SCRA = sbt("SCRA", [1, 1024], F32)
        SCRB = sbt("SCRB", [1, 1024], F32)
        EGA = sbt("EGA", [16, 1024], BF16)
        GBR = sbt("GBR", [128, 1024], BF16)
        BIASR = sbt("BIASR", [128, 1024], BF16)
        CAF = sbt("CAF", [128, 8], F32)
        SAF = sbt("SAF", [128, 8], F32)
        CBF = sbt("CBF", [128, 8], F32)
        SBF = sbt("SBF", [128, 8], F32)
        SEF = sbt("SEF", [128, 8], F32)
        SCF = sbt("SCF", [128, 8], F32)
        SCB = sbt("SCB", [128, 8], BF16)
        BTG = sbt("BTG", [128, 256], F32)
        BTV = sbt("BTV", [128, 2 * len(PV_COMBOS)], F32)
        MD = sbt("MD", [128, 128], BF16)
        ME = sbt("ME", [128, 128], BF16)
        NG = sbt("NG", [128, 128], BF16)
        IDENT = sbt("IDENT", [1, 1], F32)
        SMALL = sbt("SMALL", [1, 8], F32)
        SSV = sbt("SSV", [128, 1], F32)
        RSIN = sbt("RSIN", [128, 1], F32)
        RSK = sbt("RSK", [128, 1], F32)
        BST = sbt("BST", [128, 12], F32)
        MV = sbt("MV", [128, 2], F32)
        LNT = sbt("LNT", [128, 1], F32)
        RV = sbt("RV", [128, 1], F32)
        NMR = sbt("NMR", [128, 1], F32)
        BST2 = sbt("BST2", [128, 12], F32)
        MV2 = sbt("MV2", [128, 2], F32)
        LNT2 = sbt("LNT2", [128, 1], F32)
        RV2 = sbt("RV2", [128, 1], F32)
        NMR2 = sbt("NMR2", [128, 1], F32)
        LNSCR = ((BST, MV, LNT, RV, NMR), (BST2, MV2, LNT2, RV2, NMR2))
        EPSV = sbt("EPSV", [128, 1], F32)
        ONES1 = sbt("ONES1", [1, 128], F32)

        PS = {"PA": pst("PA", [128, 1024]),
              "PB2": pst("PB2", [128, 1024]),
              "PC": pst("PC", [128, 512]),
              "PD": pst("PD", [128, 512]),
              "PE": pst("PE", [128, 512]),
              "PF2": pst("PF2", [128, 512])}
        PA, PB2, PC, PD, PE, PF2 = (PS[k] for k in
                                    ("PA", "PB2", "PC", "PD", "PE", "PF2"))

        sem_names = (["sx%d" % i for i in range(8)] +
                     ["sw", "swm0", "swm1", "sst", "sT", "sV", "sS",
                      "syxn0", "syxn1", "syxna0", "syxna1",
                      "syu", "syl0", "syl1",
                      "syx20", "syx21", "syx2a0", "syx2a1", "symisc", "sg"])
        sems = {n: st.enter_context(nc.semaphore(n)) for n in sem_names}

        ir = IR()
        ps_readers = {k: [] for k in PS}
        ps_readers["PA0"] = []
        ps_readers["PA1"] = []
        ps_readers["PB2a"] = []
        ps_readers["PB2b"] = []
        del ps_readers["PA"]
        del ps_readers["PB2"]

        def pwrite(name):
            ws = ps_readers[name]
            ps_readers[name] = []
            return list(ws)

        def pread(name, evname):
            if evname is not None:
                ps_readers[name].append(evname)

        pb_readers = {i: [] for i in range(4)}
        et_last = [None]

        WQr = WQ.rearrange("p (k n) -> p k n", k=8)
        WFr = WF.rearrange("p (k n) -> p k n", k=8)
        KRr = KR.rearrange("p (s j r) -> p s j r", s=8, j=8)
        VRr = VR.rearrange("p (s h c) -> p s h c", s=8, h=16)
        QGr = QG.rearrange("p (t j r) -> p t j r", t=4, j=8)
        HFMr = HFM.rearrange("p (t j r) -> p t j r", t=4, j=8)
        XRr = XR.rearrange("p (s n) -> p s n", s=8)
        H2Fr = H2F.rearrange("p (b j r) -> p b j r", b=2, j=8)
        PBr = PBUF.rearrange("p (b n) -> p b n", b=4)
        UBr = UB.rearrange("p (h n) -> p h n", h=16)
        UTr = UT.rearrange("p (t c) -> p t c", t=4)
        LRr = LROW.rearrange("p (b n) -> p b n", b=2)

        xld_count = [0] * 8

        def load_x(t, waits=()):
            s = t % 8
            xld_count[s] += 1
            ir.add("G", lambda g, t=t, s=s: g.dma_start(
                out=XRr[:, s, :], in_=xin[128 * t:128 * (t + 1), :]),
                waits=waits, inc=("sx%d" % s, 16))

        for t in range(8):
            load_x(t)
        for k8 in range(8):
            ir.add("G", lambda g, k8=k8: g.dma_start(
                out=WQr[:, k8, :], in_=wkqv[128 * k8:128 * (k8 + 1), :]),
                inc=("sw", 16))
        for k8 in range(8):
            ir.add("G", lambda g, k8=k8: g.dma_start(
                out=WFr[:, k8, :], in_=fker[128 * k8:128 * (k8 + 1), :]),
                inc=("sw", 16))
        for src, dst in ((cfm_d, SCF), (btg_d, BTG), (btv_d, BTV),
                         (md_d, MD), (me_d, ME), (ng_d, NG), (id_d, IDENT)):
            ir.add("G", lambda g, src=src, dst=dst: g.dma_start(
                out=dst[:, :], in_=src), inc=("sw", 16))
        ir.add("G", lambda g: g.dma_start(out=SMALL[0:1, 0:1], in_=sin_d),
               inc=("sw", 16))
        ir.add("G", lambda g: g.dma_start(out=SMALL[0:1, 1:2], in_=sk_d),
               inc=("sw", 16))
        ir.add("G", lambda g: g.dma_start(out=SCRB[0:1, :], in_=fbias),
               inc=("sw", 16), ev="fb_row")
        w_all = ("sw", ir.cum("sw"))

        ir.add("V", lambda v: v.memset(VRr[:, :, :, 64:65], 1.0), inc=("sV", 1))
        ir.add("V", lambda v: v.memset(EPSV[:, :], 1e-5), inc=("sV", 1), ev="eps_set")
        ir.add("V", lambda v: v.memset(ONES1[0:1, :], 1.0), inc=("sV", 1), ev="ones_set")
        ir.add("V", lambda v: v.memset(ZB[:, :], 0.0), inc=("sV", 1), ev="zb_set")
        ir.add("V", lambda v: v.memset(IDB[0:1, :], 1.0), inc=("sV", 1), ev="idb_set")
        ir.add("S", lambda s: s.nop(), inc=("sS", 1))


        def replicate_row(src_row, dst, n, waits, evname):
            nmm = (n + 511) // 512
            for i in range(nmm):
                w = list(waits) + ["ones_set"] + (pwrite("PF2") if i == 0
                                                  else ["%s_e%d" % (evname, i - 1)])
                c0, c1 = 512 * i, min(n, 512 * (i + 1))
                ir.add("T", lambda te, c0=c0, c1=c1: te.matmul(
                    PF2[:, 0:c1 - c0] if False else PF2[:, c0 % 512:(c0 % 512) + (c1 - c0)],
                    ONES1[0:1, :], src_row[0:1, c0:c1],
                    start=True, stop=True),
                    waits=w, inc=("sT", 1),
                    ev="%s_mm%d" % (evname, i))
                ir.add("V", lambda v, c0=c0, c1=c1: v.tensor_copy(
                    dst[:, c0:c1], PF2[:, c0 % 512:(c0 % 512) + (c1 - c0)]),
                    waits=["%s_mm%d" % (evname, i)], inc=("sV", 1),
                    ev=evname if i == nmm - 1 else ("%s_e%d" % (evname, i)))
                pread("PF2", evname if i == nmm - 1 else ("%s_e%d" % (evname, i)))

        ir.add("S", lambda s: s.activation(SEF[:, :], SCF[:, :], AF.Exp,
                                           bias=0.0, scale=-1.0),
               waits=[w_all], inc=("sS", 1), ev="silu_e")
        ir.add("V", lambda v: v.tensor_scalar(SEF[:, :], SEF[:, :], 1.0, None,
                                              op0=ALU.add),
               waits=["silu_e"], inc=("sV", 1), ev="sil1")
        ir.add("V", lambda v: v.reciprocal(SEF[:, :], SEF[:, :]),
               waits=["sil1"], inc=("sV", 1), ev="sil2")
        ir.add("V", lambda v: v.tensor_tensor(SCF[:, :], SCF[:, :], SEF[:, :],
                                              op=ALU.mult),
               waits=["sil2"], inc=("sV", 1), ev="sil3")
        ir.add("V", lambda v: v.tensor_copy(SCB[:, :], SCF[:, :]),
               waits=["sil3"], inc=("sV", 1), ev="silu_done")
        ir.add("V", lambda v: v.reciprocal(SMALL[0:1, 2:3], SMALL[0:1, 0:1]),
               waits=[w_all], inc=("sV", 1))
        ir.add("V", lambda v: v.reciprocal(SMALL[0:1, 3:4], SMALL[0:1, 1:2]),
               inc=("sV", 1))
        ir.add("V", lambda v: v.tensor_tensor(SMALL[0:1, 4:5], SMALL[0:1, 0:1],
                                              SMALL[0:1, 1:2], op=ALU.mult),
               inc=("sV", 1), ev="small_done")
        replicate_row(SMALL[0:1, 2:3], RSIN, 1, ["small_done"], "rsin_rep")
        replicate_row(SMALL[0:1, 3:4], RSK, 1, ["small_done"], "rsk_rep")
        replicate_row(SMALL[0:1, 4:5], SSV, 1, ["small_done"], "bcast_done")
        replicate_row(SCRB, BIASR, 1024, ["fb_row"], "biasr_done")
        def emit_modsection():
            ir.add("V", lambda v: v.tensor_scalar(WF[:, :], WF[:, :], RSK[:, :],
                                                  None, op0=ALU.mult),
                   waits=["rsk_rep", w_all], inc=("sV", 1), ev="wf_scaled")

            vec_psums = [PA[0:1, 0:512], PA[0:1, 512:1024],
                         PB2[0:1, 0:512], PB2[0:1, 512:1024],
                         PC[0:1, 0:512], PD[0:1, 0:512]]

            def wmod_stage(ci):
                par = ci % 2
                return UB[:, 3072 * par:3072 * (par + 1)]

            for half in range(2):
                for k8 in range(8):
                    ci = half * 8 + k8
                    par = ci % 2
                    dw = [] if ci < 2 else ["mmv_%d" % (ci - 2)]
                    ir.add("G", lambda g, half=half, k8=k8, ci=ci: g.dma_start(
                        out=wmod_stage(ci),
                        in_=wmod[128 * k8:128 * (k8 + 1),
                                 3072 * half:3072 * (half + 1)]),
                        waits=dw, inc=("swm%d" % par, 16))
                    for j in range(6):
                        w = [("swm%d" % par, 16 * (ci // 2 + 1))]
                        if ci == 0 and j == 0:
                            w += ["silu_done", w_all]
                        last = (j == 5)
                        ir.add("T", lambda te, k8=k8, j=j, ci=ci: te.matmul(
                            vec_psums[j], SCB[:, k8:k8 + 1],
                            wmod_stage(ci)[:, 512 * j:512 * (j + 1)],
                            start=(k8 == 0), stop=(k8 == 7)),
                            waits=w, inc=("sT", 1) if last else None,
                            ev=("mmv_%d" % ci) if last else None)
                for vv in range(3):
                    vec = half * 3 + vv
                    evw = ["mmv_%d" % (half * 8 + 7)]
                    ir.add("V", lambda v, vv=vv: v.tensor_copy(
                        MODROW[0:1, 0:512], vec_psums[2 * vv]),
                        waits=evw, inc=("sV", 1))
                    ir.add("V", lambda v, vv=vv: v.tensor_copy(
                        MODROW[0:1, 512:1024], vec_psums[2 * vv + 1]),
                        inc=("sV", 1), ev="modrow_%d" % vec)
                    for kk in (("PA0", "PA1"), ("PB2a", "PB2b"), ("PC", "PD"))[vv]:
                        pread(kk, "modrow_%d" % vec)
                    if vec in (0, 1, 3, 4):
                        dstv = {0: SAF, 1: CAF, 3: SBF, 4: CBF}[vec]
                        for j in range(8):
                            wv = ["modrow_%d" % vec] if j == 0 else []
                            if j > 0:
                                wv.append("vtrj_%d_%d" % (vec, j - 1))
                            ir.add("V", lambda v, j=j: v.tensor_copy(
                                MODF[0:1, :], MODROW[0:1, 128 * j:128 * (j + 1)]),
                                waits=wv, inc=("sV", 1), ev="modf_%d_%d" % (vec, j))
                            w = ["modf_%d_%d" % (vec, j)]
                            if j == 0:
                                w += pwrite("PE")
                            ir.add("T", lambda te, j=j: te.transpose(
                                PE[:, j:j + 1], MODF[0:1, :],
                                IDENT[0:1, 0:1]),
                                waits=w, inc=("sT", 1),
                                ev="vtrj_%d_%d" % (vec, j) if j < 7 else ("vtr_%d" % vec))
                        ir.add("V", lambda v, dstv=dstv: v.tensor_copy(
                            dstv[:, :], PE[:, 0:8]),
                            waits=["vtr_%d" % vec], inc=("sV", 1),
                            ev="vfm_%d" % vec)
                        pread("PE", "vfm_%d" % vec)
                    else:
                        cw = ["modrow_%d" % vec, "biasr_done_mm1"]
                        if vec == 5:
                            cw.append("ega_done")
                        ir.add("V", lambda v: v.memset(SCRB[0:1, :], 0.0),
                               waits=cw, inc=("sV", 1), ev="cl_%d_-1" % vec)
                        cseq = [
                            lambda v: v.tensor_tensor(SCRA[0:1, :], MODROW[0:1, :],
                                                      SCRB[0:1, :], op=ALU.is_ge),
                            lambda v: v.tensor_scalar(SCRA[0:1, :], SCRA[0:1, :],
                                                      2.0, -1.0, op0=ALU.mult,
                                                      op1=ALU.add),
                            lambda v: v.tensor_scalar(SCRB[0:1, :], MODROW[0:1, :],
                                                      -1.0, None, op0=ALU.mult),
                            lambda v: v.tensor_tensor(SCRB[0:1, :], MODROW[0:1, :],
                                                      SCRB[0:1, :], op=ALU.max),
                            lambda v: v.tensor_scalar(SCRB[0:1, :], SCRB[0:1, :],
                                                      -0.25, None, op0=ALU.add),
                            lambda v: v.tensor_relu(SCRB[0:1, :], SCRB[0:1, :]),
                            lambda v: v.tensor_scalar(SCRB[0:1, :], SCRB[0:1, :],
                                                      0.25, None, op0=ALU.add),
                            lambda v: v.tensor_tensor(SCRA[0:1, :], SCRA[0:1, :],
                                                      SCRB[0:1, :], op=ALU.mult),
                        ]
                        for ci2, fn in enumerate(cseq):
                            ir.add("V", fn,
                                   waits=["cl_%d_%d" % (vec, ci2 - 1)],
                                   inc=("sV", 1),
                                   ev=("clamp_%d" % vec) if ci2 == len(cseq) - 1
                                   else "cl_%d_%d" % (vec, ci2))
                        if vec == 2:
                            ir.add("G", lambda g: g.memset(EGA[:, :], 0.0),
                                   inc=("sg", 1), ev="ega_ms")
                            for h in range(16):
                                ir.add("G", lambda g, h=h: g.dma_start(
                                    out=EGA[h:h + 1, 64 * h:64 * (h + 1)],
                                    in_=SCRA[0:1, 64 * h:64 * (h + 1)]),
                                    waits=(["clamp_2", "ega_ms", "ln1_tr_3"]
                                           if h == 0 else ()),
                                    inc=("symisc", 16),
                                    ev="ega_done" if h == 15 else None)
                        else:
                            replicate_row(SCRA, GBR, 1024, ["clamp_5"],
                                          "gbr_done")
            ir.add("V", lambda v: v.tensor_scalar(CAF[:, :], CAF[:, :], 1.0, None,
                                                  op0=ALU.add),
                   waits=["vfm_1"], inc=("sV", 1))
            ir.add("V", lambda v: v.tensor_scalar(CBF[:, :], CBF[:, :], 1.0,
                                                  RSIN[:, :], op0=ALU.add,
                                                  op1=ALU.mult),
                   waits=["vfm_4", "rsin_rep"], inc=("sV", 1))
            ir.add("V", lambda v: v.tensor_scalar(SBF[:, :], SBF[:, :], RSIN[:, :],
                                                  None, op0=ALU.mult),
                   waits=["vfm_3"], inc=("sV", 1), ev="mod_ready")

        def ln_ops(t, src_ap, dst_ap, fm_dst, syp, prefix, sid=0,
                   extra_v_waits=(), extra_sy_waits=(), guard=True):
            BST_, MV_, LNT_, RV_, NMR_ = LNSCR[sid]
            ops = []
            prev_tr = "%s_tr_%d" % (prefix, t - 1)
            xnw = [prev_tr] if (guard and prev_tr in ir.events) else []
            ops.append(lambda: ir.add(
                "V", lambda v: v.bn_stats(BST_[:, 0:6], src_ap[:, 0:512]),
                waits=list(extra_v_waits), inc=("sV", 1)))
            ops.append(lambda: ir.add(
                "V", lambda v: v.bn_stats(BST_[:, 6:12], src_ap[:, 512:1024]),
                inc=("sV", 1), ev="%s_bst_%d" % (prefix, t)))
            ops.append(lambda: ir.add(
                "V", lambda v: v.bn_aggr(MV_[:, :], BST_[:, :]),
                waits=["%s_bst_%d" % (prefix, t)],
                inc=("sV", 1), ev="%s_mv_%d" % (prefix, t)))
            ops.append(lambda: ir.add(
                "S", lambda s: s.activation(LNT_[:, :], MV_[:, 1:2], AF.Ln,
                                            bias=EPSV[:, :], scale=1.0),
                waits=["%s_mv_%d" % (prefix, t), "eps_set"], inc=("sS", 1),
                ev="%s_lnt_%d" % (prefix, t)))
            ops.append(lambda: ir.add(
                "S", lambda s: s.activation(RV_[:, :], LNT_[:, :], AF.Exp,
                                            bias=0.0, scale=-0.5),
                waits=["%s_lnt_%d" % (prefix, t)],
                inc=("sS", 1), ev="%s_rv_%d" % (prefix, t)))
            ops.append(lambda: ir.add(
                "V", lambda v: v.tensor_scalar(NMR_[:, :], MV_[:, 0:1],
                                               RV_[:, :], -1.0,
                                               op0=ALU.mult, op1=ALU.mult),
                waits=["%s_rv_%d" % (prefix, t)], inc=("sV", 1),
                ev="%s_nmr_%d" % (prefix, t)))
            prev_tra = "%s_tra_%d" % (prefix, t - 1)
            xnaw = [prev_tra] if (guard and prev_tra in ir.events) else []
            ops.append(lambda: ir.add(
                "V", lambda v: v.tensor_scalar(dst_ap[:, 0:512],
                                               src_ap[:, 0:512], RV_[:, :],
                                               NMR_[:, :], op0=ALU.mult,
                                               op1=ALU.add),
                waits=xnaw + ["%s_nmr_%d" % (prefix, t)],
                inc=("sV", 1), ev="%s_xna_%d" % (prefix, t)))
            ops.append(lambda: ir.add(
                "SY", lambda sy: sy.dma_start_transpose(
                    fm_dst[:, 0:4, :], dst_ap[:, 0:512]),
                waits=["%s_xna_%d" % (prefix, t)] + list(extra_sy_waits),
                inc=(syp[:-1] + "a" + syp[-1], 16),
                ev="%s_tra_%d" % (prefix, t)))
            ops.append(lambda: ir.add(
                "V", lambda v: v.tensor_scalar(dst_ap[:, 512:1024],
                                               src_ap[:, 512:1024], RV_[:, :],
                                               NMR_[:, :], op0=ALU.mult,
                                               op1=ALU.add),
                waits=xnw, inc=("sV", 1), ev="%s_xn_%d" % (prefix, t)))
            ops.append(lambda: ir.add(
                "SY", lambda sy: sy.dma_start_transpose(
                    fm_dst[:, 4:8, :], dst_ap[:, 512:1024]),
                waits=["%s_xn_%d" % (prefix, t)],
                inc=(syp, 16), ev="%s_tr_%d" % (prefix, t)))
            return ops

        def mod_ops(t, fm_view, cvec, svec, tr_ev, prefix):
            tra_ev = tr_ev.replace("_tr_", "_tra_")
            ops = []
            for j in range(8):
                if j == 0:
                    w = [tra_ev, "mod_ready"]
                elif j == 4:
                    w = [tr_ev]
                else:
                    w = ()
                ops.append(lambda j=j, w=w: ir.add(
                    "V", lambda v, j=j: v.tensor_scalar(
                        fm_view[:, j, :], fm_view[:, j, :],
                        cvec[:, j:j + 1], svec[:, j:j + 1],
                        op0=ALU.mult, op1=ALU.add),
                    waits=w,
                    inc=("sV", 1) if j == 7 else None,
                    ev=("%s_mod_%d" % (prefix, t)) if j == 7 else None))
            return ops

        def emit_zip(la, lb):
            for i in range(max(len(la), len(lb))):
                if i < len(la):
                    la[i]()
                if i < len(lb):
                    lb[i]()
        def ln_pipeline(t, src_ap, dst_ap, fm_dst, syp, prefix,
                        extra_v_waits=(), extra_sy_waits=()):
            for f in ln_ops(t, src_ap, dst_ap, fm_dst, syp, prefix, 0,
                            extra_v_waits, extra_sy_waits):
                f()

        def modulate(t, fm_view, cvec, svec, tr_ev, prefix):
            for f in mod_ops(t, fm_view, cvec, svec, tr_ev, prefix):
                f()

        n_store = 0

        def a_tile_ops(gg, ti, extra_sy, sid=0):
            gt0a = 4 * (gg + 1)
            t = gt0a + ti
            s = t % 8
            xw = [("sx%d" % s, 16 * xld_count[s])]
            esy = list(extra_sy)
            if t == 4:
                esy.append("ega_done")
            return (ln_ops(t, XRr[:, s, :], XN[:, :],
                           HFMr[:, ti, :, :], "syxn%d" % (t % 2), "ln1", sid,
                           xw, esy) +
                    mod_ops(t, HFMr[:, ti, :, :], CAF, SAF,
                            "ln1_tr_%d" % t, "h1"))

        def emit_A_ln(gg, ti, extra_sy):
            gt0a = 4 * (gg + 1)
            t = gt0a + ti
            s = t % 8
            xw = [("sx%d" % s, 16 * xld_count[s])]
            esy = list(extra_sy)
            if t == 4:
                esy.append("ega_done")
            ln_pipeline(t, XRr[:, s, :], XN[:, :],
                        HFMr[:, ti, :, :], "syxn%d" % (t % 2), "ln1",
                        extra_v_waits=xw, extra_sy_waits=esy)

        def emit_A_tile(gg, ti, extra_sy):
            for f in a_tile_ops(gg, ti, extra_sy):
                f()

        def emit_A(gg):
            gt0a = 4 * (gg + 1)
            extra_sy = ["Bdone_%d" % (gg - 1)] if gg >= 0 else []
            for ti in range(4):
                emit_A_ln(gg, ti, extra_sy)
            for ti in range(4):
                t = gt0a + ti
                modulate(t, HFMr[:, ti, :, :], CAF, SAF, "ln1_tr_%d" % t, "h1")

        def emit_B_kq(gb, mis):
            gt0b = 4 * (gb + 1)
            tilesb = list(range(gt0b, gt0b + 4))
            hfm_ready = ["h1_mod_%d" % t for t in tilesb]
            for m in mis:
                pgn = "PA%d" % (m % 2)
                pg = PA[:, 0:512] if (m % 2 == 0) else PA[:, 512:1024]
                for k8 in range(8):
                    w = []
                    if k8 == 0:
                        w = pwrite(pgn)
                        if m == 0:
                            w += hfm_ready + [w_all]
                    ir.add("T", lambda te, m=m, k8=k8, pg=pg: te.matmul(
                        pg, WQr[:, k8, 128 * m:128 * (m + 1)],
                        HFMr[:, :, k8, :], start=(k8 == 0), stop=(k8 == 7)),
                        waits=w, inc=("sT", 1) if k8 == 7 else None,
                        ev=("qkmm_%d_%d" % (gb, m)) if k8 == 7 else None)
                if m < 8:
                    dst = KRr[:, (gt0b % 8):(gt0b % 8) + 4, m, :]
                else:
                    dst = QGr[:, :, m - 8, :]
                evw = ["qkmm_%d_%d" % (gb, m)]
                if m == 0 and gb >= 1 and ("attn_done_%d" % (gb - 1)) in ir.events:
                    evw.append("attn_done_%d" % (gb - 1))
                ir.add("S", lambda s_, dst=dst, pg=pg: s_.copy(
                    dst, pg.rearrange("p (t r) -> p t r", t=4)),
                    waits=evw, inc=("sS", 1),
                    ev="qkev_%d_%d" % (gb, m))
                pread(pgn, "qkev_%d_%d" % (gb, m))

        def emit_B_v(gb, tis):
            gt0b = 4 * (gb + 1)
            tilesb = list(range(gt0b, gt0b + 4))
            nm = 8 if gb < 0 else 16
            for ti in tis:
                t = tilesb[ti]
                for k8 in range(8):
                    for half2 in range(2):
                        pvn = "PC" if half2 == 0 else "PD"
                        pv_ = PS[pvn]
                        w = []
                        if k8 == 0:
                            w = pwrite(pvn)
                            if half2 == 0 and ti == 0:
                                w.append("qkmm_%d_%d" % (gb, nm - 1))
                        ir.add("T", lambda te, ti=ti, k8=k8, half2=half2, pv_=pv_: te.matmul(
                            pv_[:, :],
                            HFMr[:, ti, k8, :],
                            WQr[:, k8, 2048 + 512 * half2:2048 + 512 * (half2 + 1)],
                            start=(k8 == 0), stop=(k8 == 7)),
                            waits=w,
                            inc=("sT", 1) if (k8 == 7 and half2 == 1) else None,
                            ev=("vmm_%d_%d" % (gb, ti)) if (k8 == 7 and half2 == 1) else None)
                s = t % 8
                evw = ["vmm_%d_%d" % (gb, ti)]
                if ti == 0 and gb >= 1 and ("attn_done_%d" % (gb - 1)) in ir.events:
                    evw.append("attn_done_%d" % (gb - 1))
                ir.add("S", lambda s_, s=s: s_.copy(
                    VRr[:, s, 0:8, 0:64],
                    PC.rearrange("p (h c) -> p h c", h=8)),
                    waits=evw, inc=("sS", 1))
                pread("PC", None)
                ir.add("S", lambda s_, s=s: s_.copy(
                    VRr[:, s, 8:16, 0:64],
                    PD.rearrange("p (h c) -> p h c", h=8)),
                    inc=("sS", 1), ev="vev_%d_%d" % (gb, ti))
                pread("PC", "vev_%d_%d" % (gb, ti))
                pread("PD", "vev_%d_%d" % (gb, ti))

        def emit_B_done(gb):
            nm = 8 if gb < 0 else 16
            ir.add("T", lambda te: te.nop(),
                   waits=["qkev_%d_%d" % (gb, nm - 1), "vev_%d_%d" % (gb, 3)],
                   inc=("sT", 1), ev="Bdone_%d" % gb)

        def emit_B(gb):
            nm = 8 if gb < 0 else 16
            emit_B_kq(gb, list(range(nm)))
            emit_B_v(gb, [0, 1, 2, 3])
            emit_B_done(gb)

        for ti_ in range(4):
            emit_A_ln(-1, ti_, [])
        emit_modsection()
        for ti_ in range(4):
            modulate(ti_, HFMr[:, ti_, :, :], CAF, SAF,
                     "ln1_tr_%d" % ti_, "h1")
        emit_B(-1)
        for t in range(8, 12):
            load_x(t, waits=["ln1_xn_%d" % (t - 8)])
        emit_A(0)
        emit_B(0)
        emit_A(1)

        for g in range(0, NGR):
            gt0 = 4 * (g + 1)
            tiles = list(range(gt0, gt0 + 4))
            jt0 = gt0
            last_qkev = "qkev_%d_%d" % (g, 15)


            if PHASE < 2:
                for ti, t in enumerate(tiles):
                    s = t % 8
                    n_store += 1
                    ir.add("G", lambda g_, t=t, s=s: g_.dma_start(
                        out=yout[128 * (t - 4):128 * (t - 3), :], in_=XRr[:, s, :]),
                        waits=["ln1_xn_%d" % t, "vev_%d_%d" % (g, 3)],
                        inc=("sst", 16))
                for t in range(gt0 + 8, min(gt0 + 12, NT)):
                    load_x(t, waits=[("sst", 16 * n_store)])
                continue
            pvlen = len(PV_COMBOS)

            def sc_bank(h, kt):
                i = (kt % 2) * 2 + (h % 2)
                return (("PC", PS["PC"][:, :]), ("PD", PS["PD"][:, :]),
                        ("PB2a", PB2[:, 0:512]), ("PB2b", PB2[:, 512:1024]))[i]

            def pb_blk(h, kt):
                return (kt % 2) * 2 + (h % 2)

            def head_kts(h):
                if h in PV_HEADS:
                    return list(range(gt0 - dlim(h), gt0 + 4))
                if dlim(h) == 4:
                    return ([gt0, gt0 - 3, gt0 - 2, gt0 - 1, gt0 - 4] +
                            list(range(gt0 + 1, gt0 + 4)))
                return ([gt0] + list(range(gt0 - dlim(h), gt0)) +
                        list(range(gt0 + 1, gt0 + 4)))

            av_state = {}

            def emit_zero(h, first):
                pavn = "PE" if (h % 2 == 0) else "PF2"
                pav = PS[pavn]
                zw = ["zb_set"] + pwrite(pavn)
                if first:
                    zw += [last_qkev, "vev_%d_%d" % (g, 3)]
                ir.add("T", lambda te, pav=pav, h=h, gt0=gt0: te.matmul(
                    pav[0:65, 0:512], VRr[:, gt0 % 8, h, 0:65],
                    ZB[:, :].unsqueeze(1).broadcast_to([128, 4, 128]),
                    start=True, stop=False),
                    waits=zw, inc=("sT", 1))
                av_state[h] = True

            def emit_qk(h, ki, kt, first):
                qlo = max(jt0, kt)
                qhi = min(jt0 + 3, kt + dlim(h))
                nq = qhi - qlo + 1
                q0 = qlo - jt0
                pscn, psc = sc_bank(h, kt)
                w = pwrite(pscn)
                if first:
                    w += [last_qkev, "vev_%d_%d" % (g, 3)]
                fin = "qk_%d_%d_%d" % (g, h, kt)
                ir.add("T", lambda te, h=h, kt=kt, q0=q0, nq=nq, psc=psc: te.matmul(
                    psc[:, 128 * q0:128 * (q0 + nq)],
                    KRr[64 * (h % 2):64 * (h % 2) + 64, kt % 8, h // 2, :],
                    QGr[64 * (h % 2):64 * (h % 2) + 64, q0:q0 + nq, h // 2, :],
                    start=True, stop=True),
                    waits=w, inc=("sT", 1), ev=fin)

            def emit_exp(h, ki, kt):
                qlo = max(jt0, kt)
                qhi = min(jt0 + 3, kt + dlim(h))
                nq = qhi - qlo + 1
                q0 = qlo - jt0
                pscn, psc = sc_bank(h, kt)
                fin = "qk_%d_%d_%d" % (g, h, kt)
                has_diag = jt0 <= kt <= jt0 + 3
                has_edge = dlim(h) == 4 and jt0 <= kt + 4 <= jt0 + 3
                blk = pb_blk(h, kt)
                pb = PBr[:, blk, :]
                expw = [fin] + pb_readers[blk]
                pb_readers[blk] = []
                if h in PV_HEADS:
                    for qt in range(qlo, qhi + 1):
                        dd = qt - kt
                        col = ((0 if kt < 4 else 1) * pvlen +
                               PV_COMBOS.index((h, dd)))
                        qo2 = qt - jt0
                        last = (qt == qhi)
                        ir.add("S", lambda s, psc=psc, pb=pb, qo2=qo2, col=col: s.activation(
                            pb[:, 128 * qo2:128 * (qo2 + 1)],
                            psc[:, 128 * qo2:128 * (qo2 + 1)],
                            AF.Exp, bias=BTV[:, col:col + 1],
                            scale=INV_SQRT_D),
                            waits=expw, inc=("sS", 1),
                            ev=("exps_%d_%d_%d" % (g, h, kt)) if last else None)
                        expw = []
                else:
                    opp = kt - 4 * g
                    gcase = 0 if g == 0 else 1
                    col = (gcase * 8 + opp) * NH + h
                    ir.add("S", lambda s, psc=psc, pb=pb, q0=q0, nq=nq, col=col: s.activation(
                        pb[:, 128 * q0:128 * (q0 + nq)],
                        psc[:, 128 * q0:128 * (q0 + nq)],
                        AF.Exp, bias=BTG[:, col:col + 1], scale=INV_SQRT_D),
                        waits=expw, inc=("sS", 1),
                        ev="exps_%d_%d_%d" % (g, h, kt))
                pread(pscn, "exps_%d_%d_%d" % (g, h, kt))
                mev = "exps_%d_%d_%d" % (g, h, kt)
                if has_diag or has_edge:
                    msk = MD if has_diag else ME
                    qo = (kt - jt0) if has_diag else (kt + 4 - jt0)
                    ir.add("V", lambda v, pb=pb, qo=qo, msk=msk: v.tensor_tensor(
                        pb[:, 128 * qo:128 * (qo + 1)],
                        pb[:, 128 * qo:128 * (qo + 1)],
                        msk[:, :], op=ALU.mult),
                        waits=[mev], inc=("sV", 1),
                        ev="exp_%d_%d_%d" % (g, h, kt))
                else:
                    ir.events["exp_%d_%d_%d" % (g, h, kt)] = ir.events[mev]

            def emit_av(h, kt):
                pavn = "PE" if (h % 2 == 0) else "PF2"
                pav = PS[pavn]
                qlo = max(jt0, kt)
                qhi = min(jt0 + 3, kt + dlim(h))
                nq = qhi - qlo + 1
                q0 = qlo - jt0
                pb = PBr[:, pb_blk(h, kt), :]
                avw = ["exp_%d_%d_%d" % (g, h, kt)]
                if not av_state.get(h):
                    avw += pwrite(pavn)
                    st_flag = True
                    av_state[h] = True
                else:
                    st_flag = False
                ave = "av_%d_%d_%d" % (g, h, kt)
                ir.add("T", lambda te, pav=pav, pb=pb, kt=kt, q0=q0, nq=nq, h=h,
                       st_=st_flag, sp_=(kt == gt0 + 3): te.matmul(
                    pav[0:65, 128 * q0:128 * (q0 + nq)],
                    VRr[:, kt % 8, h, 0:65],
                    pb[:, 128 * q0:128 * (q0 + nq)],
                    start=st_, stop=sp_),
                    waits=avw, inc=("sT", 1), ev=ave)
                pb_readers[pb_blk(h, kt)].append(ave)

            def emit_tail(h):
                pavn = "PE" if (h % 2 == 0) else "PF2"
                pav = PS[pavn]
                uw = ["av_%d_%d_%d" % (g, h, gt0 + 3)]
                if h <= 1 and g >= 1:
                    uw.append(("syu", 16 * 16 * g))
                ir.add("V", lambda v, h=h, pav=pav: v.tensor_copy(
                    UBr[0:64, h, :], pav[0:64, :]),
                    waits=uw, inc=("sV", 1), ev="uev_%d_%d" % (g, h))
                pread(pavn, "uev_%d_%d" % (g, h))
                lw = ["av_%d_%d_%d" % (g, h, gt0 + 3)]
                lk = "lgat_%d_%d" % (g, h - 2)
                if lk in ir.events:
                    lw.append(lk)
                elif g >= 1 and ("lgat_%d_%d" % (g - 1, h + HC - 2)) in ir.events:
                    lw.append("lgat_%d_%d" % (g - 1, h + HC - 2))
                ir.add("S", lambda s, h=h, pav=pav: s.copy(
                    LRr[64:65, h % 2, :], pav[64:65, :]),
                    waits=lw, inc=("sS", 1), ev="lev_%d_%d" % (g, h))
                pread(pavn, "lev_%d_%d" % (g, h))
                syw = ["lev_%d_%d" % (g, h)]
                if h == 0:
                    drain_t = (gt0 + 7) if (g + 1 < NGR) else tiles[3]
                    syw.append("ln1_tr_%d" % drain_t)
                    p2 = "ln2_tr_%d" % (gt0 - 1)
                    if p2 in ir.events:
                        syw.append(p2)
                    if g >= 1:
                        syw.append("recip_%d" % (g - 1))
                        syw.append(("syu", 16 * 16 * g))
                ir.add("SY", lambda sy, h=h: sy.dma_start(
                    out=LT[h:h + 1, :], in_=LRr[64:65, h % 2, :]),
                    waits=syw, inc=("syl%d" % (h % 2), 16),
                    ev="lgat_%d_%d" % (g, h))

            for hp in range(HC // 2):
                h0, h1 = 2 * hp, 2 * hp + 1
                kts0, kts1 = head_kts(h0), head_kts(h1)
                if h0 in PV_HEADS:
                    emit_zero(h0, first=(hp == 0))
                if h1 in PV_HEADS:
                    emit_zero(h1, first=False)
                n = max(len(kts0), len(kts1))
                sched = []
                for ki in range(n):
                    if ki < len(kts0):
                        sched.append((h0, ki, kts0[ki]))
                    if ki < len(kts1):
                        sched.append((h1, ki, kts1[ki]))
                pend = []
                for i, (h, ki, kt) in enumerate(sched):
                    emit_qk(h, ki, kt, first=(hp == 0 and i == 0 and
                                              h0 not in PV_HEADS))
                    blk = pb_blk(h, kt)
                    while pend and (len(pend) >= 4 or
                                    any(pb_blk(hh, kk) == blk for hh, kk in pend)):
                        hh, kk = pend.pop(0)
                        emit_av(hh, kk)
                    emit_exp(h, ki, kt)
                    pend.append((h, kt))
                for hh, kk in pend:
                    emit_av(hh, kk)
                emit_tail(h0)
                emit_tail(h1)
            ir.add("T", lambda te: te.nop(),
                   waits=["av_%d_%d_%d" % (g, HC - 1, gt0 + 3)],
                   inc=("sT", 1), ev="attn_done_%d" % g)

            rw = ["lgat_%d_%d" % (g, HC - 2 if HC >= 2 else 0),
                  "lgat_%d_%d" % (g, HC - 1)]
            pr15 = "rp_%d_%d" % (g - 1, 15)
            if pr15 in ir.events:
                rw.append(pr15)
            ir.add("V", lambda v: v.reciprocal(LT[:, :], LT[:, :]),
                   waits=rw, inc=("sV", 1), ev="recq_%d" % g)
            ir.add("S", lambda s_: s_.copy(LTR[:, :], LT[:, :]),
                   waits=["recq_%d" % g], inc=("sS", 1), ev="recip_%d" % g)

            for h in range(HC):
                prn = "PC" if (h % 2 == 0) else "PD"
                pr = PS[prn]
                w = pwrite(prn) + ["recip_%d" % g, "ega_done"]
                ir.add("T", lambda te, h=h, pr=pr: te.matmul(
                    pr[0:64, :], EGA[:, 64 * h:64 * (h + 1)], LTR[:, :],
                    start=True, stop=True),
                    waits=w, inc=("sT", 1), ev="rp_%d_%d" % (g, h))
                ir.add("V", lambda v, h=h, pr=pr: v.tensor_tensor(
                    UBr[0:64, h, :], UBr[0:64, h, :], pr[0:64, :],
                    op=ALU.mult),
                    waits=["rp_%d_%d" % (g, h), "uev_%d_%d" % (g, h)],
                    inc=("sV", 1), ev="umul_%d_%d" % (g, h))
                pread(prn, "umul_%d_%d" % (g, h))
                syw = ["umul_%d_%d" % (g, h)]
                if h == 0:
                    syw += ["lgat_%d_%d" % (g, HC - 2 if HC >= 2 else 0),
                            "lgat_%d_%d" % (g, HC - 1)]
                    if g >= 1:
                        syw.append("resid_%d_%d" % (g - 1, 3))
                    fe = "ffnep_%d_3" % (g - 1)
                    if fe in ir.events:
                        syw.append(fe)
                ir.add("SY", lambda sy, h=h: sy.dma_start_transpose(
                    UTr[:, :, 64 * h:64 * (h + 1)], UBr[0:64, h, :]),
                    waits=syw, inc=("syu", 16), ev="utr_%d_%d" % (g, h))

            for ti, t in enumerate(tiles):
                s = t % 8
                ir.add("V", lambda v, s=s, ti=ti: v.tensor_tensor(
                    XRr[:, s, :], XRr[:, s, :], UTr[:, ti, :], op=ALU.add),
                    waits=["utr_%d_%d" % (g, HC - 1)],
                    inc=("sV", 1), ev="resid_%d_%d" % (g, ti))

            if PHASE < 3:
                for ti, t in enumerate(tiles):
                    s = t % 8
                    n_store += 1
                    ir.add("G", lambda g_, t=t, s=s: g_.dma_start(
                        out=yout[128 * (t - 4):128 * (t - 3), :], in_=XRr[:, s, :]),
                        waits=["resid_%d_%d" % (g, ti)], inc=("sst", 16))
                for t in range(gt0 + 8, min(gt0 + 12, NT)):
                    load_x(t, waits=[("sst", 16 * n_store)])
                continue
            def ln2_tile_ops(ti, sid=0, buf=None, guard=True):
                t = tiles[ti]
                s = t % 8
                extra_sy = []
                pk = "ffnmm_%d_%d" % (g, ti - 2)
                if pk in ir.events:
                    extra_sy.append(pk)
                else:
                    pk2 = "ffnmm_%d_%d" % (g - 1, ti + 2)
                    if pk2 in ir.events:
                        extra_sy.append(pk2)
                buf = X2N if buf is None else buf
                return (ln_ops(t, XRr[:, s, :], buf[:, :],
                               H2Fr[:, ti % 2, :, :], "syx2%d" % (t % 2),
                               "ln2", sid,
                               ["resid_%d_%d" % (g, ti)], extra_sy, guard) +
                        mod_ops(t, H2Fr[:, ti % 2, :, :], CBF, SBF,
                                "ln2_tr_%d" % t, "h2"))

            def emit_ln2(ti):
                for f in ln2_tile_ops(ti):
                    f()

            def emit_ffn(ti):
                t = tiles[ti]
                for k8 in range(8):
                    for half2 in range(2):
                        pfn = "PE" if half2 == 0 else "PF2"
                        pf_ = PS[pfn]
                        w = []
                        if k8 == 0 and half2 == 0:
                            w = (pwrite("PE") + pwrite("PF2") +
                                 ["h2_mod_%d" % t, "wf_scaled"])
                        ir.add("T", lambda te, ti=ti, k8=k8, half2=half2, pf_=pf_: te.matmul(
                            pf_[:, :],
                            H2Fr[:, ti % 2, k8, :],
                            WFr[:, k8, 512 * half2:512 * (half2 + 1)],
                            start=(k8 == 0), stop=(k8 == 7)),
                            waits=w,
                            inc=("sT", 1) if (k8 == 7 and half2 == 1) else None,
                            ev=("ffnmm_%d_%d" % (g, ti)) if (k8 == 7 and half2 == 1) else None)

            def emit_epi(ti):
                t = tiles[ti]
                s = t % 8
                FY = PBUF[:, 0:1024]
                w = (["ffnmm_%d_%d" % (g, ti), "biasr_done", "gbr_done"] +
                     pb_readers[0] + pb_readers[1])
                pb_readers[0] = []
                pb_readers[1] = []
                ir.add("V", lambda v, FY=FY: v.tensor_scalar(
                    FY[:, 0:512], PE[:, :], SSV[:, :], None, op0=ALU.mult),
                    waits=w, inc=("sV", 1), ev="y1a_%d_%d" % (g, ti))
                pread("PE", "y1a_%d_%d" % (g, ti))
                ir.add("S", lambda s_, FY=FY: s_.activation(
                    FY[:, 512:1024], PF2[:, :], AF.Copy,
                    bias=0.0, scale=SSV[:, 0:1]),
                    waits=list(w),
                    inc=("sS", 1), ev="y1_%d_%d" % (g, ti))
                pread("PF2", "y1_%d_%d" % (g, ti))
                ir.add("V", lambda v, FY=FY: v.tensor_tensor(
                    FY, FY, BIASR[:, :], op=ALU.add),
                    waits=["y1a_%d_%d" % (g, ti), "y1_%d_%d" % (g, ti)],
                    inc=("sV", 1), ev="y2_%d_%d" % (g, ti))
                ir.add("V", lambda v, FY=FY: v.tensor_tensor(
                    UT[:, 0:1024], FY, GBR[:, :], op=ALU.mult),
                    waits=["y2_%d_%d" % (g, ti)],
                    inc=("sV", 1), ev="y3_%d_%d" % (g, ti))
                ir.add("V", lambda v, s=s: v.tensor_tensor(
                    XRr[:, s, :], XRr[:, s, :], UT[:, 0:1024], op=ALU.add),
                    waits=["y3_%d_%d" % (g, ti)],
                    inc=("sV", 1), ev="ffnep_%d_%d" % (g, ti))
                pb_readers[0].append("ffnep_%d_%d" % (g, ti))
                pb_readers[1].append("ffnep_%d_%d" % (g, ti))
                nonlocal_store(t, s, "ffnep_%d_%d" % (g, ti))

            def nonlocal_store(t, s, wev):
                ir.add("G", lambda g_, t=t, s=s: g_.dma_start(
                    out=yout[128 * (t - 4):128 * (t - 3), :], in_=XRr[:, s, :]),
                    waits=[wev], inc=("sst", 16))

            hasB = g + 1 < NGR
            hasA = g + 2 < NGR

            def load_ahead(ti):
                tl = gt0 + 8 + ti
                if tl < NT:
                    load_x(tl, waits=[("sst", 16 * n_store)])

            if hasB:
                emit_B_kq(g + 1, [0, 1, 2, 3, 4, 5])
            emit_zip(ln2_tile_ops(0),
                     ln2_tile_ops(1, sid=1, buf=XN, guard=False))
            emit_ffn(0)
            if hasB:
                emit_B_kq(g + 1, [6, 7, 8, 9, 10, 11])
            emit_epi(0)
            n_store += 1
            load_ahead(0)
            emit_ffn(1)
            if hasB:
                emit_B_kq(g + 1, [12, 13, 14, 15])
                emit_B_v(g + 1, [0])
            if hasA:
                emit_A_tile(g + 2, 0,
                            ["qkmm_%d_15" % (g + 1), "vmm_%d_0" % (g + 1)])
                emit_ln2(2)
            else:
                emit_zip(ln2_tile_ops(2),
                         ln2_tile_ops(3, sid=1, buf=XN, guard=False))
            emit_epi(1)
            n_store += 1
            load_ahead(1)
            emit_ffn(2)
            if hasB:
                emit_B_v(g + 1, [1, 2])
            if hasA:
                emit_A_tile(g + 2, 1,
                            ["qkmm_%d_15" % (g + 1), "vmm_%d_1" % (g + 1)])
                emit_ln2(3)
            emit_epi(2)
            n_store += 1
            load_ahead(2)
            emit_ffn(3)
            if hasB:
                emit_B_v(g + 1, [3])
                emit_B_done(g + 1)
            if hasA:
                emit_A_tile(g + 2, 2,
                            ["qkmm_%d_15" % (g + 1), "vmm_%d_2" % (g + 1)])
            emit_epi(3)
            n_store += 1
            load_ahead(3)
            if hasA:
                emit_A_tile(g + 2, 3,
                            ["qkmm_%d_15" % (g + 1), "vmm_%d_3" % (g + 1)])

        ir.add("G", lambda g_: g_.nop(), waits=[("sst", 16 * n_store)],
               inc=("sg", 1))

        with nc.Block() as blk:
            def emit_engine(name, dec):
                ops = [(w, i, e) for (en, w, i, e) in ir.ops if en == name]

                def body(eng, ops=ops):
                    for waits, incs, emit in ops:
                        for semn, val in waits:
                            eng.wait_ge(sems[semn], val)
                        inst = emit(eng)
                        for semn, amt in incs:
                            inst.then_inc(sems[semn], amt)
                dec(body)

            emit_engine("T", blk.tensor)
            emit_engine("V", blk.vector)
            emit_engine("S", blk.scalar)
            emit_engine("SY", blk.sync)
            emit_engine("G", blk.gpsimd)

    return nc


_NC_CACHE = None


def _get_nc():
    global _NC_CACHE
    if _NC_CACHE is None:
        _NC_CACHE = build_nc()
    return _NC_CACHE


def kernel(L0_attn_x, L0_attn_w_kqv, L0_ffn_kernel, L0_ffn_input_scale,
           L0_ffn_kernel_scale, L0_ffn_bias, L0_c, L0_wmod):
    x = np.ascontiguousarray(np.asarray(L0_attn_x, dtype=np.float32))
    wkqv = np.ascontiguousarray(np.asarray(L0_attn_w_kqv, dtype=np.float32))
    wmod_a = np.ascontiguousarray(np.asarray(L0_wmod, dtype=np.float32))
    fker = np.ascontiguousarray(np.asarray(L0_ffn_kernel).astype(np.float32))
    fbias = np.ascontiguousarray(np.asarray(L0_ffn_bias).astype(np.float32)).reshape(1, D)
    sin = np.asarray(L0_ffn_input_scale, dtype=np.float32).reshape(1, 1)
    sk = np.asarray(L0_ffn_kernel_scale, dtype=np.float32).reshape(1, 1)
    cc = np.asarray(L0_c, dtype=np.float32)

    nc = _get_nc()
    in_maps = []
    zpad = np.zeros((512, D), dtype=np.float32)
    ident1 = np.ones((1, 1), dtype=np.float32)
    negi = np.diag(np.full(128, -1.0e5, dtype=np.float32))
    tabs = [make_tables(0), make_tables(1)]
    for c in range(8):
        b, half = c // 2, c % 2
        if half == 0:
            xs = np.concatenate([zpad, x[b, 0:2048]], axis=0)
        else:
            xs = np.ascontiguousarray(x[b, 1536:4096])
        cfm = np.ascontiguousarray(cc[b].reshape(8, 128).T)
        btg, btv, mdiag, medge = tabs[half]
        in_maps.append({
            "xin": xs, "wkqv": wkqv, "wmod": wmod_a, "fker": fker,
            "fbias": fbias, "sin": sin, "sk": sk, "cfm": cfm,
            "btg": btg, "btv": btv, "mdiag": mdiag, "medge": medge,
            "negi": negi, "ident1": ident1,
        })
    res = run_bass_kernel_spmd(nc, in_maps, core_ids=list(range(8)))
    out = np.empty((B, SEQ, D), dtype=np.float32)
    for c in range(8):
        b, half = c // 2, c % 2
        out[b, half * 2048:(half + 1) * 2048] = res.results[c]["yout"]
    return out


